# version 06 (final)
# speedup vs baseline: 1.3975x; 1.1667x over previous
import numpy as np
import ml_dtypes
import concourse.bass as bass
import concourse.mybir as mybir
from concourse.bass_utils import run_bass_kernel_spmd

F32 = mybir.dt.float32
BF16 = mybir.dt.bfloat16
AF = mybir.ActivationFunctionType
ALU = mybir.AluOpType
AX = mybir.AxisListType

T = 4096
DM = 1024
NT = 32
CH = 128
NCH = 32
L0_H = 4
L1_H = 16
L1_D = 64
LN0_PARTS = ("stats", "trans", "router", "gates")


class Sem:
    def __init__(self, h):
        self.h = h
        self.n = 0

    def inc(self, k=1):
        self.n += k
        return self.n


class P:
    ENGS = ("sync", "gpsimd", "tensor", "vector", "scalar")

    def __init__(self, C):
        self.C = C
        self.steps = []

    def y(self, fn):
        self.steps.append(("sync", fn))

    def g(self, fn):
        self.steps.append(("gpsimd", fn))

    def t(self, fn):
        self.steps.append(("tensor", fn))

    def v(self, fn):
        self.steps.append(("vector", fn))

    def s(self, fn):
        self.steps.append(("scalar", fn))

    def emit(self):
        blk = self.C.block
        for name in self.ENGS:
            fns = [f for e, f in self.steps if e == name]
            if fns:
                getattr(blk, name)(lambda eng, fns=fns: [f(eng) for f in fns])
        self.steps = []


class Ctx:
    def __init__(self, nc, block, io, sb, es_top):
        self.nc = nc
        self.block = block
        self.io = io
        self.sb = sb
        self._es = es_top
        self.sem_list = []
        self._sem_names = {}

    def sem(self, name):
        if name not in self._sem_names:
            h = self._es.enter_context(self.nc.semaphore(name))
            sm = Sem(h)
            self._sem_names[name] = sm
            self.sem_list.append(sm)
        return self._sem_names[name]


def chain(p, eng, sem, ops, pre=None):
    n = len(ops)
    base = sem.inc(n)

    def run(e, ops=ops, base=base, n=n, h=sem.h, pre=pre):
        if pre is not None:
            pre(e)
        for j, op in enumerate(ops):
            w = base - n + j
            if w > 0:
                e.wait_ge(h, w)
            op(e).then_inc(h, 1)
    (p.v if eng == "v" else p.s)(run)
    return base


def barrier(C, p):
    targets = [(s.h, s.n) for s in C.sem_list if s.n]

    def w(eng, targets=targets):
        for h, n in targets:
            eng.wait_ge(h, n)
    for e in P.ENGS:
        p.steps.append((e, w))



def host_constants():
    c = {}
    c["ident"] = np.eye(128, dtype=np.float32)
    c["ones_row"] = np.ones((1, 128), np.float32)
    inv_freq = (10000.0 ** (-(np.arange(128, dtype=np.float64) / 128.0)))
    ang = np.arange(T, dtype=np.float64)[None, :] * inv_freq[:, None]
    c["cosK"] = np.cos(ang).astype(np.float32)
    c["sinK"] = np.sin(ang).astype(np.float32)
    c["cosQ"] = (np.cos(ang) / 16.0).astype(np.float32)
    c["sinQ"] = (np.sin(ang) / 16.0).astype(np.float32)
    gammas = 1.0 - np.exp2(-5.0 - np.arange(L0_H, dtype=np.float64))
    idx = np.arange(CH)
    dmT = np.zeros((L0_H, CH, CH), np.float32)
    cqt = np.zeros((L0_H, CH, 2 * CH), np.float32)
    ckt = np.zeros((CH, L0_H), np.float32)
    gci = np.zeros((L0_H, 128, 128), np.float32)
    for h in range(L0_H):
        g = gammas[h]
        rel = idx[None, :] - idx[:, None]
        dmT[h] = (np.where(rel >= 0, g ** rel, 0.0) / 16.0).astype(np.float32)
        row = (g ** (idx + 1) / 16.0).astype(np.float32)
        cqt[h] = np.broadcast_to(np.concatenate([row, row])[None, :], (CH, 2 * CH))
        ckt[:, h] = (g ** (CH - 1 - idx)).astype(np.float32)
        gci[h] = (np.eye(128) * (g ** CH)).astype(np.float32)
    c["dmT"] = dmT
    c["cqt"] = cqt
    c["ckt"] = ckt
    c["gci"] = gci
    sl = np.tril(np.ones((128, 128), np.float32), -1)
    c["maskSL"] = sl
    c["maskSU"] = np.ascontiguousarray(sl.T)
    return c


def host_prep(inputs):
    w = {}
    w["Wq0"] = np.ascontiguousarray(inputs["L0_attn_Wq"], np.float32)
    w["Wk0"] = np.ascontiguousarray(inputs["L0_attn_Wk"], np.float32)
    w["Wv0"] = np.ascontiguousarray(inputs["L0_attn_Wv"], np.float32)
    w["Wg0"] = np.ascontiguousarray(inputs["L0_attn_Wg"], np.float32)
    w["Wo0"] = np.ascontiguousarray(inputs["L0_attn_Wo"], np.float32)
    w["gn_w"] = np.ascontiguousarray(np.reshape(inputs["L0_attn_gn_weight"], (1, 512)), np.float32)
    w["w12T"] = np.ascontiguousarray(np.transpose(inputs["L0_ffn_w12"], (0, 2, 1)), np.float32)
    w["w3T"] = np.ascontiguousarray(np.transpose(inputs["L0_ffn_w3"], (0, 2, 1)), np.float32)
    w["w1T"] = np.ascontiguousarray(np.transpose(inputs["L0_ffn_shared_w1"]), np.float32)
    w["w2T"] = np.ascontiguousarray(np.transpose(inputs["L0_ffn_shared_w2"]), np.float32)
    w["w3sT"] = np.ascontiguousarray(np.transpose(inputs["L0_ffn_shared_w3"]), np.float32)
    w["routerT"] = np.ascontiguousarray(np.transpose(inputs["L0_ffn_router_w"]), np.float32)
    w["ebias"] = np.ascontiguousarray(np.reshape(inputs["L0_ffn_expert_bias"], (1, 8)), np.float32)
    w["Wq1T"] = np.ascontiguousarray(np.transpose(inputs["L1_attn_Wq"]), np.float32)
    w["Wk1T"] = np.ascontiguousarray(np.transpose(inputs["L1_attn_Wk"]), np.float32)
    w["Wv1T"] = np.ascontiguousarray(np.transpose(inputs["L1_attn_Wv"]), np.float32)
    w["Wp1T"] = np.ascontiguousarray(np.transpose(inputs["L1_attn_Wproj"]), np.float32)
    w["araw"] = np.ascontiguousarray(np.reshape(inputs["L1_attn_alpha_raw"], (1, 16)), np.float32)
    w["braw"] = np.ascontiguousarray(np.reshape(inputs["L1_attn_beta_raw"], (1, 16)), np.float32)
    w["gateT"] = np.ascontiguousarray(np.transpose(inputs["L1_ffn_gate_w"]), np.float32)
    w["gupT"] = np.ascontiguousarray(np.transpose(inputs["L1_ffn_gate_up_w"], (0, 2, 1)), np.float32)
    w["downT"] = np.ascontiguousarray(np.transpose(inputs["L1_ffn_down_w"], (0, 2, 1)), np.float32)
    return w



def ph_ln0(C, es):
    nc, io, sb = C.nc, C.io, C.sb
    p = P(C)
    ts, vs, ss = C.sem("ts"), C.sem("vs"), C.sem("ss")
    sc0 = C.sem("ln_c")
    sx = [C.sem("ln_x0"), C.sem("ln_x1")]
    xn = sb["xnD"]
    gates = sb["gates0"]
    idf = sb["identF"]

    xt = es.enter_context(nc.sbuf_tensor("ln_xt", [128, 2 * 1024], F32))
    xsq = es.enter_context(nc.sbuf_tensor("ln_xsq", [128, 1024], F32))
    st_ = es.enter_context(nc.sbuf_tensor("ln_st", [128, 8], F32))
    xnf = es.enter_context(nc.sbuf_tensor("ln_xnf", [128, 1024], F32))
    xnT = es.enter_context(nc.sbuf_tensor("ln_xnT", [128, 1024], F32))
    rT = es.enter_context(nc.sbuf_tensor("ln_rT", [128, 64], F32))
    bsml = es.enter_context(nc.sbuf_tensor("ln_b", [1, 8], F32))
    ones_r = es.enter_context(nc.sbuf_tensor("ln_ones", [1, 128], F32))
    brep = es.enter_context(nc.sbuf_tensor("ln_brep", [128, 8], F32))
    scw = es.enter_context(nc.sbuf_tensor("ln_scw", [128, 24], F32))
    pt0 = es.enter_context(nc.psum_tensor("ln_pt0", [128, 128], F32))
    pt1 = es.enter_context(nc.psum_tensor("ln_pt1", [128, 128], F32))
    psc = es.enter_context(nc.psum_tensor("ln_psc", [128, 8], F32))
    pbr = es.enter_context(nc.psum_tensor("ln_pbr", [128, 8], F32))
    pts = [pt0, pt1]

    p.y(lambda e: (
        e.dma_start(out=rT.rearrange("p (a e) -> p a e", a=8),
                    in_=io["routerT"].rearrange("(a p) e -> p a e", p=128)).then_inc(sc0.h, 16),
        e.dma_start(out=bsml[:, :], in_=io["ebias"][:, :]).then_inc(sc0.h, 16),
        e.dma_start(out=ones_r[:, :], in_=io["ones_row"][:, :]).then_inc(sc0.h, 16)))
    w0 = sc0.inc(48)
    p.t(lambda e: (e.wait_ge(sc0.h, w0),
                   e.matmul(pbr[:, :], ones_r[:, :], bsml[:, :], start=True, stop=True).then_inc(ts.h, 1)))
    w1 = ts.inc(1)
    brep_rdy = chain(p, "v", vs,
                     [lambda e: e.tensor_copy(brep[:, :], pbr[:, :])],
                     pre=lambda e, w1=w1: e.wait_ge(ts.h, w1))

    xnf_done, var_done, trans_last, evs_last, sc_done, vg_done = {}, {}, {}, {}, {}, {}
    for i in range(NT):
        slot = i % 2
        xti = xt[:, slot * 1024:(slot + 1) * 1024]

        def load(e, i=i, slot=slot, xti=xti):
            if i >= 2:
                e.wait_ge(vs.h, xnf_done[i - 2])
                e.wait_ge(ss.h, evs_last[i - 2])
            e.dma_start(out=xti, in_=io["x"][i * 128:(i + 1) * 128, :]).then_inc(sx[slot].h, 16)
        p.y(load)
        ld = sx[slot].inc(16)

        def pre_ssq(e, ld=ld, i=i, slot=slot):
            e.wait_ge(sx[slot].h, ld)
            if i >= 1:
                e.wait_ge(vs.h, var_done[i - 1])
        ssq_d = chain(p, "s", ss,
                      [lambda e, xti=xti: e.activation(xsq[:, :], xti, AF.Square,
                                                       accum_out=st_[:, 5:6])],
                      pre=pre_ssq)

        var_done[i] = chain(p, "v", vs, [
            lambda e, xti=xti: e.reduce_sum(st_[:, 0:1], xti, axis=AX.X, negate=True),
            lambda e: e.tensor_scalar(st_[:, 0:1], st_[:, 0:1], 1.0 / DM, None, ALU.mult),
            lambda e: e.tensor_scalar(st_[:, 1:2], st_[:, 0:1], st_[:, 0:1], 1e-5, ALU.mult, ALU.subtract),
            lambda e, ssq_d=ssq_d: (e.wait_ge(ss.h, ssq_d),
                e.tensor_scalar(st_[:, 2:3], st_[:, 5:6], 1.0 / DM, st_[:, 1:2],
                                ALU.mult, ALU.subtract))[1],
        ], pre=lambda e, ld=ld, slot=slot: e.wait_ge(sx[slot].h, ld))

        r_d = chain(p, "s", ss, [
            lambda e: e.activation(st_[:, 3:4], st_[:, 2:3], AF.Ln, bias=0.0, scale=1.0),
            lambda e: e.activation(st_[:, 4:5], st_[:, 3:4], AF.Exp, bias=0.0, scale=-0.5),
        ], pre=lambda e, vd=var_done[i]: e.wait_ge(vs.h, vd))

        def pre_xnf(e, r_d=r_d, i=i):
            e.wait_ge(ss.h, r_d)
            if i >= 1 and trans_last[i - 1]:
                e.wait_ge(ts.h, trans_last[i - 1])
        xnf_done[i] = chain(p, "v", vs,
                            [lambda e, xti=xti: e.tensor_scalar(
                                xnf[:, :], xti, st_[:, 0:1], st_[:, 4:5], ALU.add, ALU.mult)],
                            pre=pre_xnf)

        ev_v, ev_s = {}, {}
        prev_evs = getattr(C, "_ln_prev_evs", None)
        for a in range(8):
            pa = pts[a % 2]

            def tr(e, a=a, pa=pa, i=i, ev_v=ev_v, ev_s=ev_s):
                e.wait_ge(vs.h, xnf_done[i])
                if a >= 2:
                    e.wait_ge(vs.h, ev_v[a - 2])
                e.transpose(pa[:, :], xnf[:, a * 128:(a + 1) * 128], idf[:, :]).then_inc(ts.h, 1)
            p.t(tr)
            trd = ts.inc(1)

            def pre_evv(e, trd=trd, a=a, prev_evs=prev_evs):
                e.wait_ge(ts.h, trd)
                if prev_evs is not None:
                    e.wait_ge(ss.h, prev_evs[a])
            ev_v[a] = chain(p, "v", vs,
                            [lambda e, a=a, pa=pa: e.tensor_copy(
                                xnT[:, a * 128:(a + 1) * 128], pa[:, :])],
                            pre=pre_evv)
            ev_s[a] = chain(p, "s", ss,
                            [lambda e, a=a, i=i: e.copy(
                                xn[:, a * T + i * 128:a * T + (i + 1) * 128],
                                xnT[:, a * 128:(a + 1) * 128])],
                            pre=lambda e, w=ev_v[a]: e.wait_ge(vs.h, w))
        trans_last[i] = ts.n
        evs_last[i] = ss.n
        C._ln_prev_evs = dict(ev_s)

        allv = vs.n

        def rmm(e, allv=allv, i=i):
            e.wait_ge(vs.h, allv)
            if i >= 1:
                e.wait_ge(ss.h, sc_done[i - 1])
            for a in range(8):
                ins = e.matmul(psc[:, :], xnT[:, a * 128:(a + 1) * 128], rT[:, a * 8:(a + 1) * 8],
                               start=(a == 0), stop=(a == 7))
            ins.then_inc(ts.h, 1)
        p.t(rmm)
        mm_d = ts.inc(1)

        def pre_sexp(e, mm_d=mm_d, i=i):
            e.wait_ge(ts.h, mm_d)
            if i >= 1:
                e.wait_ge(vs.h, vg_done[i - 1])
        sc_done[i] = chain(p, "s", ss,
                           [lambda e: e.activation(scw[:, 8:16], psc[:, :], AF.Exp,
                                                   bias=0.0, scale=-1.0)],
                           pre=pre_sexp)

        def pre_vg(e, e_d=sc_done[i], i=i):
            e.wait_ge(ss.h, e_d)
            if i == 0:
                e.wait_ge(vs.h, brep_rdy)
        vg_done[i] = chain(p, "v", vs, [
            lambda e: e.tensor_scalar(scw[:, 8:16], scw[:, 8:16], 1.0, None, ALU.add),
            lambda e: e.reciprocal(scw[:, 0:8], scw[:, 8:16]),
            lambda e: e.tensor_tensor(scw[:, 16:24], scw[:, 0:8], brep[:, :], ALU.add),
            lambda e: e.reduce_max(st_[:, 6:7], scw[:, 16:24], axis=AX.X),
            lambda e: e.tensor_single_scalar(scw[:, 8:16], scw[:, 16:24], st_[:, 6:7], ALU.is_ge),
            lambda e: e.scalar_tensor_tensor(scw[:, 8:16], scw[:, 8:16], -1e30, scw[:, 16:24],
                                             ALU.mult, ALU.add),
            lambda e: e.reduce_max(st_[:, 7:8], scw[:, 8:16], axis=AX.X),
            lambda e: e.tensor_single_scalar(scw[:, 8:16], scw[:, 16:24], st_[:, 7:8], ALU.is_ge),
            lambda e, i=i: e.tensor_tensor(gates[:, i * 8:(i + 1) * 8], scw[:, 0:8],
                                           scw[:, 8:16], ALU.mult),
        ], pre=pre_vg)

    p.emit()



def ph_qk(C, es):
    nc, io, sb = C.nc, C.io, C.sb
    p = P(C)
    ts, vs = C.sem("ts"), C.sem("vs")
    sw = C.sem("qk_w")
    stab = C.sem("qk_tab")
    st = [C.sem("qk_st0"), C.sem("qk_st1")]
    xn = sb["xnD"]

    wqk = es.enter_context(nc.sbuf_tensor("qk_w", [128, 2 * 8192], BF16))
    cq_ = es.enter_context(nc.sbuf_tensor("qk_cq", [128, T], F32))
    sq_ = es.enter_context(nc.sbuf_tensor("qk_sq", [128, T], F32))
    ck_ = es.enter_context(nc.sbuf_tensor("qk_ck", [128, T], F32))
    sk_ = es.enter_context(nc.sbuf_tensor("qk_sk", [128, T], F32))
    rtmp = es.enter_context(nc.sbuf_tensor("qk_rt", [128, 2 * 512], F32))
    ostg = es.enter_context(nc.sbuf_tensor("qk_os", [128, 2 * 1024], BF16))
    pq = [es.enter_context(nc.psum_tensor(f"qk_p{j}", [128, 512], F32)) for j in range(4)]

    p.g(lambda e: (
        e.dma_start(out=wqk[:, 0:8192].rearrange("p (k n) -> p k n", k=8),
                    in_=io["Wq0"].rearrange("(k p) n -> p k n", p=128)).then_inc(sw.h, 16),
        e.dma_start(out=wqk[:, 8192:16384].rearrange("p (k n) -> p k n", k=8),
                    in_=io["Wk0"].rearrange("(k p) n -> p k n", p=128)).then_inc(sw.h, 16)))
    p.y(lambda e: (
        e.dma_start(out=cq_[:, :], in_=io["cosQ"][:, :]).then_inc(stab.h, 16),
        e.dma_start(out=sq_[:, :], in_=io["sinQ"][:, :]).then_inc(stab.h, 16),
        e.dma_start(out=ck_[:, :], in_=io["cosK"][:, :]).then_inc(stab.h, 16),
        e.dma_start(out=sk_[:, :], in_=io["sinK"][:, :]).then_inc(stab.h, 16)))
    w_rdy = sw.inc(32)
    tab_rdy = stab.inc(64)

    rope_done = {}
    for wi, (wbase, cosb, sinb, dst) in enumerate(
            [(0, cq_, sq_, "qs"), (8192, ck_, sk_, "ks")]):
        for tsl in range(8):
            for h in range(L0_H):
                it = wi * 32 + tsl * 4 + h
                slot = it % 2
                pa, pb = pq[2 * slot], pq[2 * slot + 1]

                def mm(e, it=it, wbase=wbase, tsl=tsl, h=h, pa=pa, pb=pb):
                    if it == 0:
                        e.wait_ge(sw.h, w_rdy)
                        e.wait_ge(stab.h, tab_rdy)
                    if it >= 2:
                        e.wait_ge(vs.h, rope_done[it - 2])
                    for o, pp in ((2 * h, pa), (2 * h + 1, pb)):
                        for k in range(8):
                            ins = e.matmul(pp[:, :],
                                           wqk[:, wbase + k * 1024 + o * 128: wbase + k * 1024 + (o + 1) * 128],
                                           xn[:, k * T + tsl * 512: k * T + (tsl + 1) * 512],
                                           start=(k == 0), stop=(k == 7))
                    ins.then_inc(ts.h, 1)
                p.t(mm)
                mm_d = ts.inc(1)

                def pre_rope(e, it=it, mm_d=mm_d, slot=slot):
                    e.wait_ge(ts.h, mm_d)
                    if it >= 2:
                        e.wait_ge(st[slot].h, 32 * (it // 2))
                cs0 = tsl * 512
                rope_done[it] = chain(p, "v", vs, [
                    lambda e, pa=pa, cosb=cosb, cs0=cs0: e.tensor_tensor(
                        rtmp[:, 0:512], pa[:, :], cosb[:, cs0:cs0 + 512], ALU.mult),
                    lambda e, pb=pb, sinb=sinb, cs0=cs0: e.tensor_tensor(
                        rtmp[:, 512:1024], pb[:, :], sinb[:, cs0:cs0 + 512], ALU.mult),
                    lambda e, slot=slot: e.tensor_tensor(
                        ostg[:, slot * 1024:slot * 1024 + 512],
                        rtmp[:, 0:512], rtmp[:, 512:1024], ALU.subtract),
                    lambda e, pb=pb, cosb=cosb, cs0=cs0: e.tensor_tensor(
                        rtmp[:, 0:512], pb[:, :], cosb[:, cs0:cs0 + 512], ALU.mult),
                    lambda e, pa=pa, sinb=sinb, cs0=cs0: e.tensor_tensor(
                        rtmp[:, 512:1024], pa[:, :], sinb[:, cs0:cs0 + 512], ALU.mult),
                    lambda e, slot=slot: e.tensor_tensor(
                        ostg[:, slot * 1024 + 512:slot * 1024 + 1024],
                        rtmp[:, 0:512], rtmp[:, 512:1024], ALU.add),
                ], pre=pre_rope)

                def store(e, it=it, tsl=tsl, h=h, dst=dst, slot=slot, rd=rope_done[it]):
                    e.wait_ge(vs.h, rd)
                    c0 = tsl * 4
                    for half, o in ((0, 2 * h), (1, 2 * h + 1)):
                        e.dma_start(
                            out=io[dst][c0:c0 + 4, :, o * 128:(o + 1) * 128].rearrange("c p b -> p c b"),
                            in_=ostg[:, slot * 1024 + half * 512: slot * 1024 + (half + 1) * 512]
                                .rearrange("p (c b) -> p c b", c=4)).then_inc(st[slot].h, 16)
                p.y(store)
                st[slot].inc(32)
    p.emit()



def ph_vg(C, es):
    nc, io, sb = C.nc, C.io, C.sb
    p = P(C)
    ts, vs, ss = C.sem("ts"), C.sem("vs"), C.sem("ss")
    sw = C.sem("vg_w")
    st = [C.sem("vg_st0"), C.sem("vg_st1")]
    xn = sb["xnD"]

    wvg = es.enter_context(nc.sbuf_tensor("vg_w", [128, 2 * 16384], BF16))
    vstg = es.enter_context(nc.sbuf_tensor("vg_vs", [128, 2 * 2048], BF16))
    gstg = es.enter_context(nc.sbuf_tensor("vg_gs", [128, 2 * 2048], BF16))
    pv = [es.enter_context(nc.psum_tensor(f"vg_pv{j}", [128, 512], F32)) for j in range(4)]
    pg = [es.enter_context(nc.psum_tensor(f"vg_pg{j}", [128, 512], F32)) for j in range(4)]

    p.g(lambda e: (
        e.dma_start(out=wvg[:, 0:16384].rearrange("p (k n) -> p k n", k=8),
                    in_=io["Wv0"].rearrange("(k p) n -> p k n", p=128)).then_inc(sw.h, 16),
        e.dma_start(out=wvg[:, 16384:32768].rearrange("p (k n) -> p k n", k=8),
                    in_=io["Wg0"].rearrange("(k p) n -> p k n", p=128)).then_inc(sw.h, 16)))
    w_rdy = sw.inc(32)

    ev_done = {}
    for tt in range(NT):
        slot = tt % 2

        def mm(e, tt=tt):
            if tt == 0:
                e.wait_ge(sw.h, w_rdy)
            if tt >= 1:
                vsn, ssn = ev_done[tt - 1]
                e.wait_ge(vs.h, vsn)
                e.wait_ge(ss.h, ssn)
            for k in range(8):
                lhs = xn[:, k * T + tt * 128:k * T + (tt + 1) * 128]
                for sl in range(4):
                    e.matmul(pv[sl][:, :], lhs,
                             wvg[:, k * 2048 + sl * 512:k * 2048 + (sl + 1) * 512],
                             start=(k == 0), stop=(k == 7))
                for sl in range(4):
                    ins = e.matmul(pg[sl][:, :], lhs,
                                   wvg[:, 16384 + k * 2048 + sl * 512:16384 + k * 2048 + (sl + 1) * 512],
                                   start=(k == 0), stop=(k == 7))
            ins.then_inc(ts.h, 1)
        p.t(mm)
        mm_d = ts.inc(1)

        def pre_ev(e, tt=tt, mm_d=mm_d, slot=slot):
            e.wait_ge(ts.h, mm_d)
            if tt >= 2:
                e.wait_ge(st[slot].h, 32 * (tt // 2))
        v_d = chain(p, "v", vs,
                    [(lambda e, sl=sl, slot=slot: e.tensor_copy(
                        vstg[:, slot * 2048 + sl * 512:slot * 2048 + (sl + 1) * 512],
                        pv[sl][:, :])) for sl in range(4)],
                    pre=pre_ev)
        s_d = chain(p, "s", ss,
                    [(lambda e, sl=sl, slot=slot: e.activation(
                        gstg[:, slot * 2048 + sl * 512:slot * 2048 + (sl + 1) * 512],
                        pg[sl][:, :], AF.Silu)) for sl in range(4)],
                    pre=pre_ev)
        ev_done[tt] = (v_d, s_d)

        def store(e, tt=tt, slot=slot, v_d=v_d, s_d=s_d):
            e.wait_ge(vs.h, v_d)
            e.wait_ge(ss.h, s_d)
            e.dma_start(out=io["vsd"][tt, :, :],
                        in_=vstg[:, slot * 2048:(slot + 1) * 2048]).then_inc(st[slot].h, 16)
            e.dma_start(out=io["gsd"][tt, :, :],
                        in_=gstg[:, slot * 2048:(slot + 1) * 2048]).then_inc(st[slot].h, 16)
        p.y(store)
        st[slot].inc(32)
    p.emit()



def ph_ret(C, es):
    nc, io, sb = C.nc, C.io, C.sb
    p = P(C)
    ts, vs, ss = C.sem("ts"), C.sem("vs"), C.sem("ss")
    sc0 = C.sem("rt_c")
    scg = C.sem("rt_cg")
    sld = [C.sem("rt_l0"), C.sem("rt_l1")]
    sst = [C.sem("rt_s0"), C.sem("rt_s1")]
    idb = sb["identB"]

    wo = es.enter_context(nc.sbuf_tensor("rt_wo", [128, 16 * 1024], BF16))
    dm = es.enter_context(nc.sbuf_tensor("rt_dm", [128, 4 * 128], F32))
    cq = es.enter_context(nc.sbuf_tensor("rt_cq", [128, 4 * 256], F32))
    ck = es.enter_context(nc.sbuf_tensor("rt_ck", [128, 4], F32))
    gci = es.enter_context(nc.sbuf_tensor("rt_gci", [128, 4 * 128], BF16))
    gnw = es.enter_context(nc.sbuf_tensor("rt_gnw", [1, 512], F32))
    onesr = es.enter_context(nc.sbuf_tensor("rt_ones", [1, 128], F32))
    gnrep = es.enter_context(nc.sbuf_tensor("rt_gnrep", [128, 512], F32))
    epsc = es.enter_context(nc.sbuf_tensor("rt_eps", [128, 1], F32))
    qb = es.enter_context(nc.sbuf_tensor("rt_qb", [128, 2 * 1024], BF16))
    kb = es.enter_context(nc.sbuf_tensor("rt_kb", [128, 2 * 1024], BF16))
    vb = es.enter_context(nc.sbuf_tensor("rt_vb", [128, 2 * 2048], BF16))
    gb = es.enter_context(nc.sbuf_tensor("rt_gb", [128, 2 * 2048], BF16))
    at_sb = es.enter_context(nc.sbuf_tensor("rt_at", [128, 128], BF16))
    ktm = es.enter_context(nc.sbuf_tensor("rt_ktm", [128, 256], BF16))
    kck = es.enter_context(nc.sbuf_tensor("rt_kck", [128, 256], BF16))
    qcq = es.enter_context(nc.sbuf_tensor("rt_qcq", [128, 256], BF16))
    Ssb = es.enter_context(nc.sbuf_tensor("rt_S", [128, 8 * 512], BF16))
    o_sb = es.enter_context(nc.sbuf_tensor("rt_o", [128, 512], F32))
    osq = es.enter_context(nc.sbuf_tensor("rt_osq", [128, 512], F32))
    ost = es.enter_context(nc.sbuf_tensor("rt_ost", [128, 4], F32))
    onr = es.enter_context(nc.sbuf_tensor("rt_onr", [128, 512], F32))
    o_f = es.enter_context(nc.sbuf_tensor("rt_of", [128, 512], BF16))
    oT = es.enter_context(nc.sbuf_tensor("rt_oT", [128, 16 * 128], BF16))
    s0stg = es.enter_context(nc.sbuf_tensor("rt_s0", [128, 2 * 1024], BF16))
    p_at = es.enter_context(nc.psum_tensor("rt_pat", [128, 128], F32))
    p_at2 = es.enter_context(nc.psum_tensor("rt_pat2", [128, 128], F32))
    p_tr = es.enter_context(nc.psum_tensor("rt_ptr", [128, 128], BF16))
    p_tr2 = es.enter_context(nc.psum_tensor("rt_ptr2", [128, 128], BF16))
    p_o = es.enter_context(nc.psum_tensor("rt_po", [128, 512], F32))
    p_o2 = es.enter_context(nc.psum_tensor("rt_po2", [128, 512], F32))
    p_s0_ = es.enter_context(nc.psum_tensor("rt_ps0", [128, 512], F32))
    p_s1_ = es.enter_context(nc.psum_tensor("rt_ps1", [128, 512], F32))
    p_sh = [p_s0_, p_s1_]
    p_wh = [p_s0_, p_s1_]
    p_gn = p_s0_

    p.y(lambda e: (
        e.dma_start(out=dm.rearrange("p (h n) -> p h n", h=4),
                    in_=io["dmT"].rearrange("h p n -> p h n")).then_inc(sc0.h, 16),
        e.dma_start(out=cq.rearrange("p (h n) -> p h n", h=4),
                    in_=io["cqt"].rearrange("h p n -> p h n")).then_inc(sc0.h, 16),
        e.dma_start(out=ck[:, :], in_=io["ckt"][:, :]).then_inc(sc0.h, 16),
        e.dma_start(out=gnw[:, :], in_=io["gn_w"][:, :]).then_inc(sc0.h, 16),
        e.dma_start(out=onesr[:, :], in_=io["ones_row"][:, :]).then_inc(sc0.h, 16)))
    p.g(lambda e: (
        e.dma_start(out=gci.rearrange("p (h n) -> p h n", h=4),
                    in_=io["gci"].rearrange("h p n -> p h n")).then_inc(scg.h, 16),
        e.dma_start(out=wo.rearrange("p (k n) -> p k n", k=16),
                    in_=io["Wo0"].rearrange("(k p) n -> p k n", p=128)).then_inc(scg.h, 16)))
    c_rdy = sc0.inc(80)
    g_rdy = scg.inc(32)
    p.t(lambda e: (e.wait_ge(sc0.h, c_rdy),
                   e.matmul(p_gn[:, :], onesr[:, :], gnw[:, :], start=True, stop=True).then_inc(ts.h, 1)))
    gn_mm = ts.inc(1)
    gn_rdy = chain(p, "v", vs, [
        lambda e: e.tensor_copy(gnrep[:, :], p_gn[:, :]),
        lambda e: e.memset(epsc[:, :], 1e-5),
        lambda e: e.memset(Ssb[:, :], 0.0),
    ], pre=lambda e: e.wait_ge(ts.h, gn_mm))

    RWAYS = []
    for wi in range(2):
        ce = "v" if wi == 0 else "s"
        RW = {
            "ce": ce,
            "csem": vs if ce == "v" else ss,
            "cp": (lambda e, o, i: e.tensor_copy(o, i)) if ce == "v" else (lambda e, o, i: e.copy(o, i)),
            "at_sb": es.enter_context(nc.sbuf_tensor(f"rt_at{wi}", [128, 128], BF16)),
            "ktm": es.enter_context(nc.sbuf_tensor(f"rt_ktm{wi}", [128, 256], BF16)),
            "kck": es.enter_context(nc.sbuf_tensor(f"rt_kck{wi}", [128, 256], BF16)),
            "qcq": es.enter_context(nc.sbuf_tensor(f"rt_qcq{wi}", [128, 256], BF16)),
            "o_sb": es.enter_context(nc.sbuf_tensor(f"rt_o{wi}", [128, 512], F32)),
            "osq": es.enter_context(nc.sbuf_tensor(f"rt_osq{wi}", [128, 512], F32)),
            "ost": es.enter_context(nc.sbuf_tensor(f"rt_ost{wi}", [128, 4], F32)),
            "onr": es.enter_context(nc.sbuf_tensor(f"rt_onr{wi}", [128, 512], F32)),
            "o_f": es.enter_context(nc.sbuf_tensor(f"rt_of{wi}", [128, 512], BF16)),
            "p_at": [p_at, p_at2][wi],
            "p_o": [p_o, p_o2][wi],
            "p_tr": [p_tr, p_tr2][wi],
            "at_ev": None, "ktm_ev": None, "o_ev": None, "oT_ev": None,
        }
        RWAYS.append(RW)
    C._rt_s_ev = None

    chunk_tail = {}
    s0_ev = {}
    S_ev = {}
    for c in range(NCH):
        slot = c % 2

        def load(e, c=c, slot=slot):
            if c >= 2:
                tse, vse, sse = chunk_tail[c - 2]
                e.wait_ge(ts.h, tse)
                e.wait_ge(vs.h, vse)
                e.wait_ge(ss.h, sse)
            e.dma_start(out=qb[:, slot * 1024:(slot + 1) * 1024], in_=io["qs"][c, :, :]).then_inc(sld[slot].h, 16)
            e.dma_start(out=kb[:, slot * 1024:(slot + 1) * 1024], in_=io["ks"][c, :, :]).then_inc(sld[slot].h, 16)
            e.dma_start(out=vb[:, slot * 2048:(slot + 1) * 2048], in_=io["vsd"][c, :, :]).then_inc(sld[slot].h, 16)
            e.dma_start(out=gb[:, slot * 2048:(slot + 1) * 2048], in_=io["gsd"][c, :, :]).then_inc(sld[slot].h, 16)
        p.y(load)
        ld = sld[slot].inc(64)

        def rgen(c, h, W):
            slot = c % 2
            qh = qb[:, slot * 1024 + 2 * h * 128: slot * 1024 + (2 * h + 2) * 128]
            kh0 = kb[:, slot * 1024 + 2 * h * 128: slot * 1024 + (2 * h + 1) * 128]
            kh1 = kb[:, slot * 1024 + (2 * h + 1) * 128: slot * 1024 + (2 * h + 2) * 128]
            vh = vb[:, slot * 2048 + h * 512: slot * 2048 + (h + 1) * 512]
            gh = gb[:, slot * 2048 + h * 512: slot * 2048 + (h + 1) * 512]
            ce, csem, cp = W["ce"], W["csem"], W["cp"]
            p_at, p_o, p_tr = W["p_at"], W["p_o"], W["p_tr"]

            def mm_at(e, c=c, h=h, ld=ld, slot=slot, qh=qh, kh0=kh0, kh1=kh1, at_prev=W["at_ev"]):
                e.wait_ge(sld[slot].h, ld)
                if c == 0 and h < 2:
                    e.wait_ge(scg.h, g_rdy)
                    e.wait_ge(vs.h, gn_rdy)
                if at_prev is not None:
                    e.wait_ge(vs.h, at_prev)
                e.matmul(p_at[:, :], kh0, qh[:, 0:128], start=True, stop=False)
                e.matmul(p_at[:, :], kh1, qh[:, 128:256], start=False, stop=True).then_inc(ts.h, 1)
            p.t(mm_at)
            at_mm = ts.inc(1)
            at_ev = chain(p, "v", vs, [
                lambda e, h=h: e.tensor_tensor(W["at_sb"][:, :], p_at[:, :],
                                               dm[:, h * 128:(h + 1) * 128], ALU.mult)],
                pre=lambda e, at_mm=at_mm: e.wait_ge(ts.h, at_mm))
            W["at_ev"] = at_ev
            yield

            def mm_ktr(e, kh0=kh0, ktm_prev=W["ktm_ev"]):
                if ktm_prev is not None:
                    e.wait_ge(ktm_prev[0], ktm_prev[1])
                e.transpose(p_tr[:, :], kh0, idb[:, :]).then_inc(ts.h, 1)
            p.t(mm_ktr)
            ktr0 = ts.inc(1)
            ev0 = chain(p, ce, csem, [lambda e: cp(e, W["ktm"][:, 0:128], p_tr[:, :])],
                        pre=lambda e, w=ktr0: e.wait_ge(ts.h, w))
            yield

            def mm_ktr1(e, kh1=kh1, ev0=ev0):
                e.wait_ge(csem.h, ev0)
                e.transpose(p_tr[:, :], kh1, idb[:, :]).then_inc(ts.h, 1)
            p.t(mm_ktr1)
            ktr1 = ts.inc(1)
            ev1 = chain(p, ce, csem, [lambda e: cp(e, W["ktm"][:, 128:256], p_tr[:, :])],
                        pre=lambda e, w=ktr1: e.wait_ge(ts.h, w))
            ktm_ev = chain(p, "v", vs, [
                lambda e, h=h: e.tensor_scalar(W["kck"][:, :], W["ktm"][:, :], ck[:, h:h + 1], None, ALU.mult),
                lambda e, h=h, qh=qh: e.tensor_tensor(W["qcq"][:, :], qh,
                                                      cq[:, h * 256:(h + 1) * 256], ALU.mult),
            ], pre=lambda e, w=ev1: e.wait_ge(csem.h, w))
            W["ktm_ev"] = (csem.h, ev1)
            yield

            def mm_o(e, c=c, h=h, vh=vh, ktm_ev=ktm_ev, at_ev=at_ev, o_prev=W["o_ev"]):
                e.wait_ge(vs.h, ktm_ev)
                e.wait_ge(vs.h, at_ev)
                if c > 0:
                    sm = S_ev[(c - 1, h)]
                    e.wait_ge(sm[0], sm[1])
                if o_prev is not None:
                    e.wait_ge(vs.h, o_prev)
                e.matmul(p_o[:, :], W["at_sb"][:, :], vh, start=True, stop=False)
                e.matmul(p_o[:, :], W["qcq"][:, 0:128], Ssb[:, (2 * h) * 512:(2 * h + 1) * 512],
                         start=False, stop=False)
                e.matmul(p_o[:, :], W["qcq"][:, 128:256], Ssb[:, (2 * h + 1) * 512:(2 * h + 2) * 512],
                         start=False, stop=True).then_inc(ts.h, 1)
            p.t(mm_o)
            o_mm = ts.inc(1)
            v1 = chain(p, "v", vs, [
                lambda e: e.tensor_copy(W["o_sb"][:, :], p_o[:, :]),
                lambda e: e.tensor_tensor(W["osq"][:, :], W["o_sb"][:, :], W["o_sb"][:, :], ALU.mult),
                lambda e: e.reduce_sum(W["ost"][:, 0:1], W["osq"][:, :], axis=AX.X),
            ], pre=lambda e, w=o_mm: e.wait_ge(ts.h, w))
            W["o_ev"] = v1
            yield

            def mm_s(e, c=c, h=h, vh=vh, ktm_ev=ktm_ev, s_prev=C._rt_s_ev):
                e.wait_ge(vs.h, ktm_ev)
                if s_prev is not None:
                    e.wait_ge(s_prev[0], s_prev[1])
                for hh in range(2):
                    e.matmul(p_sh[hh][:, :], gci[:, h * 128:(h + 1) * 128],
                             Ssb[:, (2 * h + hh) * 512:(2 * h + hh + 1) * 512],
                             start=True, stop=False)
                    e.matmul(p_sh[hh][:, :], W["kck"][:, hh * 128:(hh + 1) * 128], vh,
                             start=False, stop=True).then_inc(ts.h, 1)
            p.t(mm_s)
            s_mm = ts.inc(2)
            sev = chain(p, ce, csem, [
                (lambda e, h=h, hh=hh: cp(
                    e, Ssb[:, (2 * h + hh) * 512:(2 * h + hh + 1) * 512], p_sh[hh][:, :]))
                for hh in range(2)],
                pre=lambda e, w=s_mm: e.wait_ge(ts.h, w))
            S_ev[(c, h)] = (csem.h, sev)
            C._rt_s_ev = (csem.h, sev)
            yield

            r_d = chain(p, "s", ss, [
                lambda e: e.activation(W["ost"][:, 1:2], W["ost"][:, 0:1], AF.Ln,
                                       bias=epsc[:, 0:1], scale=1.0 / 512),
                lambda e: e.activation(W["ost"][:, 2:3], W["ost"][:, 1:2], AF.Exp, bias=0.0, scale=-0.5),
            ], pre=lambda e, w=v1: e.wait_ge(vs.h, w))
            of_ev = chain(p, "v", vs, [
                lambda e: e.tensor_scalar(W["onr"][:, :], W["o_sb"][:, :], W["ost"][:, 2:3], None, ALU.mult),
                lambda e: e.tensor_tensor(W["onr"][:, :], W["onr"][:, :], gnrep[:, :], ALU.mult),
                lambda e, gh=gh: e.tensor_tensor(W["o_f"][:, :], W["onr"][:, :], gh, ALU.mult),
            ], pre=lambda e, w=r_d: e.wait_ge(ss.h, w))
            yield

            for q4 in range(4):
                def mm_otr(e, q4=q4, of_ev=of_ev, oT_prev=W["ktm_ev"]):
                    if q4 == 0:
                        e.wait_ge(vs.h, of_ev)
                    if oT_prev is not None:
                        e.wait_ge(oT_prev[0], oT_prev[1])
                    e.transpose(p_tr[:, :], W["o_f"][:, q4 * 128:(q4 + 1) * 128], idb[:, :]).then_inc(ts.h, 1)
                p.t(mm_otr)
                otr = ts.inc(1)
                otev = chain(p, ce, csem, [
                    lambda e, h=h, q4=q4: cp(
                        e, oT[:, (h * 4 + q4) * 128:(h * 4 + q4 + 1) * 128], p_tr[:, :])],
                    pre=lambda e, w=otr: e.wait_ge(ts.h, w))
                W["ktm_ev"] = (csem.h, otev)
                yield
            W["oT_ev"] = (csem.h, otev)

        for pair in ((0, 1), (2, 3)):
            alive = [rgen(c, pair[0], RWAYS[0]), rgen(c, pair[1], RWAYS[1])]
            while alive:
                nxt = []
                for g in alive:
                    try:
                        next(g)
                        nxt.append(g)
                    except StopIteration:
                        pass
                alive = nxt

        oT0 = RWAYS[0]["oT_ev"]
        oT1 = RWAYS[1]["oT_ev"]

        def mm_wo(e, c=c, oT0=oT0, oT1=oT1, sprev=C._rt_s_ev):
            e.wait_ge(oT0[0], oT0[1])
            e.wait_ge(oT1[0], oT1[1])
            if sprev is not None:
                e.wait_ge(sprev[0], sprev[1])
            if c > 0:
                e.wait_ge(vs.h, s0_ev[c - 1])
            for half in range(2):
                for kk in range(16):
                    ins = e.matmul(p_wh[half][:, :], oT[:, kk * 128:(kk + 1) * 128],
                                   wo[:, kk * 1024 + half * 512: kk * 1024 + (half + 1) * 512],
                                   start=(kk == 0), stop=(kk == 15))
            ins.then_inc(ts.h, 1)
        p.t(mm_wo)
        wo_mm = ts.inc(1)

        def pre_s0(e, c=c, wo_mm=wo_mm, slot=slot):
            e.wait_ge(ts.h, wo_mm)
            if c >= 2:
                e.wait_ge(sst[slot].h, 16 * (c // 2))
        s0_ev[c] = chain(p, "v", vs, [
            lambda e, slot=slot, half=half: e.tensor_copy(
                s0stg[:, slot * 1024 + half * 512: slot * 1024 + (half + 1) * 512],
                p_wh[half][:, :])
            for half in range(2)],
            pre=pre_s0)

        C._rt_s_ev = (vs.h, s0_ev[c])

        def store(e, c=c, slot=slot, w=s0_ev[c]):
            e.wait_ge(vs.h, w)
            e.dma_start(out=io["s0s"][c, :, :],
                        in_=s0stg[:, slot * 1024:(slot + 1) * 1024]).then_inc(sst[slot].h, 16)
        p.y(store)
        sst[slot].inc(16)

        chunk_tail[c] = (ts.n, vs.n, ss.n)
    p.emit()



def ph_xdump(C, es):
    p = P(C)
    sd = C.sem("xd_st")
    xn = C.sb["xnD"]

    def dump(e):
        for a in range(8):
            e.dma_start(out=C.io["xns"][a, :, :],
                        in_=xn[:, a * T:(a + 1) * T]).then_inc(sd.h, 16)
    p.y(dump)
    sd.inc(16 * 8)
    p.emit()



def ph_ffn0(C, es):
    nc, io, sb = C.nc, C.io, C.sb
    p = P(C)
    ts, vs, ss = C.sem("ts"), C.sem("vs"), C.sem("ss")
    sxn = C.sem("f0_xn")
    sw12 = [C.sem("f0_w12a"), C.sem("f0_w12b")]
    sw3s = C.sem("f0_w3s")
    swe = C.sem("f0_we")
    sxl = C.sem("f0_x")
    ss0 = C.sem("f0_s0")
    sst = C.sem("f0_st")
    gates = sb["gates0"]

    xn_tc = es.enter_context(nc.sbuf_tensor("f0_xn", [128, 8192], BF16))
    facc = es.enter_context(nc.sbuf_tensor("f0_facc", [128, 8192], F32))
    sil = es.enter_context(nc.sbuf_tensor("f0_sil", [128, 2048], BF16))
    tgr = es.enter_context(nc.sbuf_tensor("f0_tgr", [128, 1024], F32))
    xt = es.enter_context(nc.sbuf_tensor("f0_xt", [128, 1024], F32))
    s0t = es.enter_context(nc.sbuf_tensor("f0_s0t", [128, 1024], BF16))
    x1t = es.enter_context(nc.sbuf_tensor("f0_x1t", [128, 1024], F32))
    xn1t = es.enter_context(nc.sbuf_tensor("f0_xn1t", [128, 1024], F32))
    sgt = es.enter_context(nc.sbuf_tensor("f0_sg", [128, 1024], F32))
    xsq = es.enter_context(nc.sbuf_tensor("f0_xsq", [128, 1024], F32))
    st_ = es.enter_context(nc.sbuf_tensor("f0_stat", [128, 8], F32))
    epsc = es.enter_context(nc.sbuf_tensor("f0_eps", [128, 1], F32))
    p4 = [es.enter_context(nc.psum_tensor(f"f0_p4{j}", [128, 512], F32)) for j in range(4)]
    pb = [es.enter_context(nc.psum_tensor(f"f0_pb{j}", [128, 512], F32)) for j in range(2)]
    pr = [es.enter_context(nc.psum_tensor(f"f0_pr{j}", [128, 512], F32)) for j in range(2)]

    eps_rdy = chain(p, "v", vs, [lambda e: e.memset(epsc[:, :], 1e-5)])
    p4b = [pb[0], pb[1], pr[0], pr[1]]
    A_ev = {"v": None, "s": None}
    prev = {"tc_ts": None, "B_ts": None, "D_ss": None, "D_vs": None,
            "pr_vs": None, "pb_vs": {}, "w12_use": {}, "wC_use": {},
            "xn_use": None, "w3s_use": None, "w3e_use": None,
            "x_use": {}, "s0_use": {}, "st_cnt": 0}

    from contextlib import ExitStack
    for tc in range(4):
        esA = ExitStack()
        hsh = esA.enter_context(nc.sbuf_tensor(f"f0_hsh{tc}", [128, 32768], BF16))
        wsl = esA.enter_context(nc.sbuf_tensor(f"f0_wsl{tc}", [128, 2 * 2048], BF16))
        w3sh = esA.enter_context(nc.sbuf_tensor(f"f0_w3sh{tc}", [128, 32 * 512], BF16))
        def xload(e, tc=tc, w=prev["tc_ts"]):
            if w:
                e.wait_ge(ts.h, w)
            e.dma_start(out=xn_tc.rearrange("p (a n) -> p a n", a=8),
                        in_=io["xns"].rearrange("a p n -> p a n")[:, :, tc * 1024:(tc + 1) * 1024]
                        ).then_inc(sxn.h, 16)
        p.y(xload)
        xn_rdy = sxn.inc(16)

        A_ev2 = {"v": None, "s": None}
        tc_vs0, tc_ss0 = vs.n, ss.n
        for ht in range(32):
            slot = ht % 2
            pset = p4 if ht % 2 == 0 else p4b

            def wload(e, tc=tc, ht=ht, slot=slot, w=prev["w12_use"].get(slot),
                      ptc=prev["tc_ts"], pdv=prev["D_vs"]):
                if w:
                    e.wait_ge(ts.h, w)
                if ht < 2:
                    if ptc:
                        e.wait_ge(ts.h, ptc)
                    if pdv:
                        e.wait_ge(vs.h, pdv)
                e.dma_start(out=wsl[:, slot * 2048:slot * 2048 + 1024].rearrange("p (k n) -> p k n", k=8),
                            in_=io["w1T"].rearrange("(k p) n -> p k n", p=128)[:, :, ht * 128:(ht + 1) * 128]
                            ).then_inc(sw12[slot].h, 16)
                e.dma_start(out=wsl[:, slot * 2048 + 1024:(slot + 1) * 2048].rearrange("p (k n) -> p k n", k=8),
                            in_=io["w2T"].rearrange("(k p) n -> p k n", p=128)[:, :, ht * 128:(ht + 1) * 128]
                            ).then_inc(sw12[slot].h, 16)
            p.g(wload)
            wld = sw12[slot].inc(32)

            def mmA(e, ht=ht, slot=slot, wld=wld, xn_rdy=xn_rdy, pset=pset,
                    aev=dict(A_ev2), bts=prev["B_ts"], pvs=prev.get("tc_vs"), pss=prev.get("tc_ss")):
                e.wait_ge(sw12[slot].h, wld)
                if ht == 0:
                    e.wait_ge(sxn.h, xn_rdy)
                if ht < 2 and pvs:
                    e.wait_ge(vs.h, pvs)
                    e.wait_ge(ss.h, pss)
                if aev["v"]:
                    e.wait_ge(vs.h, aev["v"])
                    e.wait_ge(ss.h, aev["s"])
                if ht < 2 and bts:
                    e.wait_ge(ts.h, bts)
                for wofs, pbase in ((0, 0), (1024, 2)):
                    for tn in range(2):
                        for k in range(8):
                            ins = e.matmul(pset[pbase + tn][:, :],
                                           wsl[:, slot * 2048 + wofs + k * 128: slot * 2048 + wofs + (k + 1) * 128],
                                           xn_tc[:, k * 1024 + tn * 512: k * 1024 + (tn + 1) * 512],
                                           start=(k == 0), stop=(k == 7))
                ins.then_inc(ts.h, 1)
            p.t(mmA)
            mmd = ts.inc(1)

            sA = chain(p, "s", ss, [
                lambda e, pset=pset, slot=slot: e.activation(
                    sil[:, slot * 1024:slot * 1024 + 512], pset[0][:, :], AF.Silu),
                lambda e, pset=pset, slot=slot: e.activation(
                    sil[:, slot * 1024 + 512:slot * 1024 + 1024], pset[1][:, :], AF.Silu),
            ], pre=lambda e, w=mmd: e.wait_ge(ts.h, w))
            vA = chain(p, "v", vs, [
                lambda e, ht=ht, pset=pset, slot=slot: e.tensor_tensor(
                    hsh[:, ht * 1024:ht * 1024 + 512],
                    sil[:, slot * 1024:slot * 1024 + 512], pset[2][:, :], ALU.mult),
                lambda e, ht=ht, pset=pset, slot=slot: e.tensor_tensor(
                    hsh[:, ht * 1024 + 512:(ht + 1) * 1024],
                    sil[:, slot * 1024 + 512:slot * 1024 + 1024], pset[3][:, :], ALU.mult),
            ], pre=lambda e, w=sA: e.wait_ge(ss.h, w))
            A_ev2 = A_ev
            A_ev = {"v": vA, "s": sA}
            prev["w12_use"][slot] = mmd
        A_last_v = A_ev["v"]

        for half in range(2):
            def w3load(e, half=half, w=prev["w3s_use"]):
                if w:
                    e.wait_ge(ts.h, w)
                e.dma_start(out=w3sh.rearrange("p (kk n) -> p kk n", kk=32),
                            in_=io["w3sT"].rearrange("(kk p) n -> p kk n", p=128)[:, :, half * 512:(half + 1) * 512]
                            ).then_inc(sw3s.h, 16)
            p.g(w3load)
            w3ld = sw3s.inc(16)

            for tsub in range(8):
                pbk = pb[tsub % 2]

                def mmB(e, half=half, tsub=tsub, pbk=pbk, w3ld=w3ld,
                        alv=A_last_v, al2=A_ev2["v"], al2s=A_ev2["s"],
                        pbv=prev["pb_vs"].get(tsub % 2)):
                    e.wait_ge(sw3s.h, w3ld)
                    if half == 0 and tsub == 0:
                        e.wait_ge(vs.h, alv)
                        if al2:
                            e.wait_ge(vs.h, al2)
                            e.wait_ge(ss.h, al2s)
                    if pbv:
                        e.wait_ge(vs.h, pbv)
                    for kk in range(32):
                        ins = e.matmul(pbk[:, :], hsh[:, kk * 1024 + tsub * 128: kk * 1024 + tsub * 128 + 128],
                                       w3sh[:, kk * 512:(kk + 1) * 512],
                                       start=(kk == 0), stop=(kk == 31))
                    ins.then_inc(ts.h, 1)
                p.t(mmB)
                mmd = ts.inc(1)

                def preB(e, w=mmd, dss=prev["D_ss"], dvs=prev["D_vs"], first=(half == 0 and tsub == 0)):
                    e.wait_ge(ts.h, w)
                    if first and dss:
                        e.wait_ge(ss.h, dss)
                        e.wait_ge(vs.h, dvs)
                prev["pb_vs"][tsub % 2] = chain(p, "v", vs, [
                    lambda e, tsub=tsub, half=half, pbk=pbk: e.tensor_copy(
                        facc[:, tsub * 1024 + half * 512: tsub * 1024 + half * 512 + 512], pbk[:, :])],
                    pre=preB)
            prev["w3s_use"] = ts.n
        prev["B_ts"] = ts.n
        esA.close()
        esC = ExitStack()
        he = esC.enter_context(nc.sbuf_tensor(f"f0_he{tc}", [128, 8192], BF16))
        w3e = esC.enter_context(nc.sbuf_tensor(f"f0_w3e{tc}", [128, 8192], BF16))
        wslC = esC.enter_context(nc.sbuf_tensor(f"f0_wslC{tc}", [128, 2 * 2048], BF16))
        g0b = esC.enter_context(nc.sbuf_tensor(f"f0_g0{tc}", [128, 8192], BF16))

        C_ev = {"v": None, "s": None}
        C_ev2 = {"v": None, "s": None}
        for ex in range(8):
            def w3eload(e, ex=ex, tc=tc, w=prev["w3e_use"], bts=prev["B_ts"]):
                if w:
                    e.wait_ge(ts.h, w)
                if ex == 0:
                    e.wait_ge(ts.h, bts)
                e.dma_start(out=w3e.rearrange("p (hp n) -> p hp n", hp=8),
                            in_=io["w3T"][ex].rearrange("(hp p) n -> p hp n", p=128)).then_inc(swe.h, 16)
            p.g(w3eload)
            w3eld = swe.inc(16)

            for hp in range(8):
                slot = hp % 2
                pset = p4 if hp % 2 == 0 else p4b

                def wCload(e, ex=ex, hp=hp, slot=slot, w=prev["wC_use"].get(slot),
                           bts=prev["B_ts"], first=(ex == 0 and hp < 2)):
                    if w:
                        e.wait_ge(ts.h, w)
                    if first:
                        e.wait_ge(ts.h, bts)
                    e.dma_start(out=wslC[:, slot * 2048:slot * 2048 + 1024].rearrange("p (k n) -> p k n", k=8),
                                in_=io["w12T"][ex].rearrange("(k p) n -> p k n", p=128)[:, :, hp * 128:(hp + 1) * 128]
                                ).then_inc(sw12[slot].h, 16)
                    e.dma_start(out=wslC[:, slot * 2048 + 1024:(slot + 1) * 2048].rearrange("p (k n) -> p k n", k=8),
                                in_=io["w12T"][ex].rearrange("(k p) n -> p k n", p=128)[:, :, 1024 + hp * 128:1024 + (hp + 1) * 128]
                                ).then_inc(sw12[slot].h, 16)
                p.g(wCload)
                wld = sw12[slot].inc(32)

                def mmC1(e, hp=hp, slot=slot, wld=wld, pset=pset,
                         cev=dict(C_ev2), aev=dict(A_ev), a2=dict(A_ev2),
                         pbv=dict(prev["pb_vs"]), prv=prev["pr_vs"], first=(hp < 2)):
                    e.wait_ge(sw12[slot].h, wld)
                    if cev["v"]:
                        e.wait_ge(vs.h, cev["v"])
                        e.wait_ge(ss.h, cev["s"])
                    else:
                        if aev["v"]:
                            e.wait_ge(vs.h, aev["v"])
                            e.wait_ge(ss.h, aev["s"])
                        if a2["v"]:
                            e.wait_ge(vs.h, a2["v"])
                            e.wait_ge(ss.h, a2["s"])
                        for v in pbv.values():
                            e.wait_ge(vs.h, v)
                    if first and prv:
                        e.wait_ge(vs.h, prv)
                    for wofs, pbase in ((0, 0), (1024, 2)):
                        for tn in range(2):
                            for k in range(8):
                                ins = e.matmul(pset[pbase + tn][:, :],
                                               wslC[:, slot * 2048 + wofs + k * 128: slot * 2048 + wofs + (k + 1) * 128],
                                               xn_tc[:, k * 1024 + tn * 512: k * 1024 + (tn + 1) * 512],
                                               start=(k == 0), stop=(k == 7))
                    ins.then_inc(ts.h, 1)
                p.t(mmC1)
                mmd = ts.inc(1)
                prev["wC_use"][slot] = mmd

                sC = chain(p, "s", ss, [
                    lambda e, pset=pset, slot=slot: e.activation(
                        sil[:, slot * 1024:slot * 1024 + 512], pset[0][:, :], AF.Silu),
                    lambda e, pset=pset, slot=slot: e.activation(
                        sil[:, slot * 1024 + 512:slot * 1024 + 1024], pset[1][:, :], AF.Silu),
                ], pre=lambda e, w=mmd: e.wait_ge(ts.h, w))
                vC = chain(p, "v", vs, [
                    lambda e, hp=hp, pset=pset, slot=slot: e.tensor_tensor(
                        he[:, hp * 1024:hp * 1024 + 512],
                        sil[:, slot * 1024:slot * 1024 + 512], pset[2][:, :], ALU.mult),
                    lambda e, hp=hp, pset=pset, slot=slot: e.tensor_tensor(
                        he[:, hp * 1024 + 512:(hp + 1) * 1024],
                        sil[:, slot * 1024 + 512:slot * 1024 + 1024], pset[3][:, :], ALU.mult),
                ], pre=lambda e, w=sC: e.wait_ge(ss.h, w))
                C_ev2 = C_ev
                C_ev = {"v": vC, "s": sC}
            A_ev = {"v": None, "s": None}
            A_ev2 = {"v": None, "s": None}

            for tsub in range(8):
                def mmC2(e, ex=ex, tsub=tsub, w3eld=w3eld, cv=C_ev["v"], c2v=C_ev2["v"],
                         c2s=C_ev2["s"], pv=prev["pr_vs"]):
                    e.wait_ge(swe.h, w3eld)
                    if tsub == 0:
                        e.wait_ge(vs.h, cv)
                        if c2v:
                            e.wait_ge(vs.h, c2v)
                            e.wait_ge(ss.h, c2s)
                    if pv:
                        e.wait_ge(vs.h, pv)
                    for half in range(2):
                        for hp in range(8):
                            ins = e.matmul(pr[half][:, :],
                                           he[:, hp * 1024 + tsub * 128: hp * 1024 + tsub * 128 + 128],
                                           w3e[:, hp * 1024 + half * 512: hp * 1024 + half * 512 + 512],
                                           start=(hp == 0), stop=(hp == 7))
                    ins.then_inc(ts.h, 1)
                p.t(mmC2)
                mmd = ts.inc(1)

                gcol = (tc * 8 + tsub) * 8 + ex
                sg_d = chain(p, "s", ss, [
                    lambda e, gcol=gcol: e.activation(tgr[:, 0:512], pr[0][:, :], AF.Copy,
                                                      scale=gates[:, gcol:gcol + 1]),
                    lambda e, gcol=gcol: e.activation(tgr[:, 512:1024], pr[1][:, :], AF.Copy,
                                                      scale=gates[:, gcol:gcol + 1]),
                ], pre=lambda e, w=mmd: e.wait_ge(ts.h, w))
                prev["pr_vs"] = chain(p, "v", vs, [
                    lambda e, tsub=tsub: e.tensor_tensor(
                        facc[:, tsub * 1024:tsub * 1024 + 1024],
                        facc[:, tsub * 1024:tsub * 1024 + 1024], tgr[:, 0:1024], ALU.add)],
                    pre=lambda e, w=sg_d: e.wait_ge(ss.h, w))
            prev["w3e_use"] = ts.n

        routed_done = prev["pr_vs"]
        d1 = chain(p, "s", ss, [
            (lambda e, tsub=tsub: e.activation(g0b[:, tsub * 1024:(tsub + 1) * 1024],
                                               facc[:, tsub * 1024:(tsub + 1) * 1024], AF.Silu))
            for tsub in range(8)],
            pre=lambda e, w=routed_done: e.wait_ge(vs.h, w))
        d2_ss = d1
        d2_vs = routed_done
        for tsub in range(8):
            cid = tc * 8 + tsub

            def dload(e, cid=cid, w=prev.get("x1_v")):
                if w:
                    e.wait_ge(vs.h, w)
                e.dma_start(out=xt[:, :], in_=io["x"][cid * 128:(cid + 1) * 128, :]).then_inc(sxl.h, 16)
                e.dma_start(out=s0t[:, :], in_=io["s0s"][cid, :, :]).then_inc(ss0.h, 16)
            p.y(dload)
            xld = sxl.inc(16)
            s0ld = ss0.inc(16)

            def preD2(e, xld=xld, s0ld=s0ld, w=d1, st_prev=prev["st_cnt"]):
                e.wait_ge(sxl.h, xld)
                e.wait_ge(ss0.h, s0ld)
                e.wait_ge(ss.h, w)
                if st_prev:
                    e.wait_ge(sst.h, st_prev)
            v_x1 = chain(p, "v", vs, [
                lambda e, tsub=tsub: e.tensor_tensor(sgt[:, :], s0t[:, :],
                                                     g0b[:, tsub * 1024:(tsub + 1) * 1024], ALU.mult),
                lambda e: e.tensor_tensor(x1t[:, :], xt[:, :], sgt[:, :], ALU.add),
            ], pre=preD2)
            prev["x1_v"] = v_x1

            sq_d = chain(p, "s", ss, [
                lambda e: e.activation(xsq[:, :], x1t[:, :], AF.Square, accum_out=st_[:, 5:6]),
            ], pre=lambda e, w=v_x1: e.wait_ge(vs.h, w))
            var_d = chain(p, "v", vs, [
                lambda e: e.reduce_sum(st_[:, 0:1], x1t[:, :], axis=AX.X, negate=True),
                lambda e: e.tensor_scalar(st_[:, 0:1], st_[:, 0:1], 1.0 / DM, None, ALU.mult),
                lambda e: e.tensor_scalar(st_[:, 1:2], st_[:, 0:1], st_[:, 0:1], 1e-5, ALU.mult, ALU.subtract),
                lambda e, w=sq_d: (e.wait_ge(ss.h, w),
                    e.tensor_scalar(st_[:, 2:3], st_[:, 5:6], 1.0 / DM, st_[:, 1:2],
                                    ALU.mult, ALU.subtract))[1],
            ])
            r_d = chain(p, "s", ss, [
                lambda e: e.activation(st_[:, 3:4], st_[:, 2:3], AF.Ln, bias=0.0, scale=1.0),
                lambda e: e.activation(st_[:, 4:5], st_[:, 3:4], AF.Exp, bias=0.0, scale=-0.5),
            ], pre=lambda e, w=var_d: e.wait_ge(vs.h, w))
            xn1_d = chain(p, "v", vs, [
                lambda e: e.tensor_scalar(xn1t[:, :], x1t[:, :], st_[:, 0:1], st_[:, 4:5],
                                          ALU.add, ALU.mult),
            ], pre=lambda e, w=r_d: e.wait_ge(ss.h, w))

            def dstore(e, cid=cid, w=xn1_d):
                e.wait_ge(vs.h, w)
                e.dma_start(out=io["x1s"][cid * 128:(cid + 1) * 128, :], in_=x1t[:, :]).then_inc(sst.h, 16)
                e.dma_start(out=io["xn1s"][cid * 128:(cid + 1) * 128, :], in_=xn1t[:, :]).then_inc(sst.h, 16)
            p.y(dstore)
            prev["st_cnt"] = sst.inc(32)
            d2_ss = r_d
            d2_vs = xn1_d
        prev["D_ss"] = d2_ss
        prev["D_vs"] = d2_vs
        prev["tc_ts"] = ts.n
        prev["tc_vs"] = vs.n
        prev["tc_ss"] = ss.n
        esC.close()
    p.emit()



def ph_l1a(C, es):
    nc, io, sb = C.nc, C.io, C.sb
    p = P(C)
    ts, vs, ss = C.sem("ts"), C.sem("vs"), C.sem("ss")
    sxl = C.sem("l1_x")
    swl = C.sem("l1_w")
    scl = C.sem("l1_c")
    sxd = C.sem("l1_xd")
    sxc = C.sem("l1_xc")
    sstt = [C.sem("l1_st0"), C.sem("l1_st1")]
    idf, idb = sb["identF"], sb["identB"]
    cw = sb["combine1w"]

    xt = es.enter_context(nc.sbuf_tensor("l1_xt", [128, 2 * 1024], F32))
    xnTf = es.enter_context(nc.sbuf_tensor("l1_xnT", [128, 1024], F32))
    xdstg = es.enter_context(nc.sbuf_tensor("l1_xd", [128, 1024], BF16))
    gT = es.enter_context(nc.sbuf_tensor("l1_gT", [128, 64], F32))
    lbuf = es.enter_context(nc.sbuf_tensor("l1_lb", [128, 8], F32))
    ebuf = es.enter_context(nc.sbuf_tensor("l1_eb", [128, 8], F32))
    mwork = es.enter_context(nc.sbuf_tensor("l1_mw", [128, 20], F32))
    raw = es.enter_context(nc.sbuf_tensor("l1_raw", [1, 32], F32))
    onesr = es.enter_context(nc.sbuf_tensor("l1_ones", [1, 128], F32))
    ab = es.enter_context(nc.sbuf_tensor("l1_ab", [1, 32], F32))
    alR = es.enter_context(nc.sbuf_tensor("l1_alR", [128, 48], F32))
    msl = es.enter_context(nc.sbuf_tensor("l1_msl", [128, 128], F32))
    msu = es.enter_context(nc.sbuf_tensor("l1_msu", [128, 128], F32))
    wqkv = es.enter_context(nc.sbuf_tensor("l1_wqkv", [128, 3 * 8192], BF16))
    wpj = es.enter_context(nc.sbuf_tensor("l1_wpj", [128, 8192], BF16))
    xnc = es.enter_context(nc.sbuf_tensor("l1_xnc", [128, 2 * 1024], BF16))
    Qsb = es.enter_context(nc.sbuf_tensor("l1_Q", [128, 1024], BF16))
    Kf = es.enter_context(nc.sbuf_tensor("l1_Kf", [128, 1024], F32))
    Ksq = es.enter_context(nc.sbuf_tensor("l1_Ksq", [128, 1024], F32))
    kst = es.enter_context(nc.sbuf_tensor("l1_kst", [128, 32], F32))
    khvb = es.enter_context(nc.sbuf_tensor("l1_R0", [128, 2048], BF16))
    WAYS = []
    for wi in range(4):
        W = {
            "khT": es.enter_context(nc.sbuf_tensor(f"l1_khT{wi}", [64, 128], BF16)),
            "qT": es.enter_context(nc.sbuf_tensor(f"l1_qT{wi}", [64, 128], BF16)),
            "qtT": es.enter_context(nc.sbuf_tensor(f"l1_qtT{wi}", [64, 128], BF16)),
            "tmpf": es.enter_context(nc.sbuf_tensor(f"l1_tmpf{wi}", [128, 128], F32)),
            "nb": es.enter_context(nc.sbuf_tensor(f"l1_nb{wi}", [128, 512], BF16)),
            "ra": es.enter_context(nc.sbuf_tensor(f"l1_ra{wi}", [128, 128], BF16)),
            "rb": es.enter_context(nc.sbuf_tensor(f"l1_rb{wi}", [128, 128], BF16)),
            "sqT": es.enter_context(nc.sbuf_tensor(f"l1_sqT{wi}", [128, 128], BF16)),
            "Qt": es.enter_context(nc.sbuf_tensor(f"l1_Qt{wi}", [128, 64], BF16)),
            "tmq": es.enter_context(nc.sbuf_tensor(f"l1_tmq{wi}", [128, 64], F32)),
            "Asb": es.enter_context(nc.sbuf_tensor(f"l1_A{wi}", [64, 64], BF16)),
            "tmA": es.enter_context(nc.sbuf_tensor(f"l1_tmA{wi}", [64, 64], F32)),
            "pg_v": None,
            "ptr_v": None,
            "ce": "v" if wi % 2 == 0 else "s",
        }
        WAYS.append(W)
    SMall = es.enter_context(nc.sbuf_tensor("l1_SM", [64, 1024], BF16))
    Ych = es.enter_context(nc.sbuf_tensor("l1_Y", [128, 1024], BF16))
    s1stg = es.enter_context(nc.sbuf_tensor("l1_s1", [128, 2 * 1024], BF16))
    pf = [es.enter_context(nc.psum_tensor(f"l1_pf{j}", [128, 512], F32)) for j in range(6)]
    pt2 = [es.enter_context(nc.psum_tensor(f"l1_pt{j}", [128, 128], BF16)) for j in range(2)]
    pp0, pp1 = pf[0], pf[1]
    pg = pf[4]
    PTRS = [None, None]
    for wi in range(4):
        WAYS[wi]["pg"] = pf[2 + wi][:, 0:128]
        WAYS[wi]["pab"] = pf[2 + wi][:, 128:384]
        WAYS[wi]["pr"] = pf[2 + wi][:, 384:512]
        WAYS[wi]["ptr"] = pt2[wi % 2]
        WAYS[wi]["pb"] = wi % 2

    p.y(lambda e: (
        e.dma_start(out=gT.rearrange("p (a e) -> p a e", a=8),
                    in_=io["gateT"].rearrange("(a p) e -> p a e", p=128)).then_inc(scl.h, 16),
        e.dma_start(out=raw[:, 0:16], in_=io["araw"][:, :]).then_inc(scl.h, 16),
        e.dma_start(out=raw[:, 16:32], in_=io["braw"][:, :]).then_inc(scl.h, 16),
        e.dma_start(out=onesr[:, :], in_=io["ones_row"][:, :]).then_inc(scl.h, 16),
        e.dma_start(out=msl[:, :], in_=io["maskSL"][:, :]).then_inc(scl.h, 16),
        e.dma_start(out=msu[:, :], in_=io["maskSU"][:, :]).then_inc(scl.h, 16)))
    c_rdy = scl.inc(96)
    p.g(lambda e: (
        e.dma_start(out=wqkv[:, 0:8192].rearrange("p (k n) -> p k n", k=8),
                    in_=io["Wq1T"].rearrange("(k p) n -> p k n", p=128)).then_inc(swl.h, 16),
        e.dma_start(out=wqkv[:, 8192:16384].rearrange("p (k n) -> p k n", k=8),
                    in_=io["Wk1T"].rearrange("(k p) n -> p k n", p=128)).then_inc(swl.h, 16),
        e.dma_start(out=wqkv[:, 16384:24576].rearrange("p (k n) -> p k n", k=8),
                    in_=io["Wv1T"].rearrange("(k p) n -> p k n", p=128)).then_inc(swl.h, 16),
        e.dma_start(out=wpj.rearrange("p (k n) -> p k n", k=8),
                    in_=io["Wp1T"].rearrange("(k p) n -> p k n", p=128)).then_inc(swl.h, 16)))
    w_rdy = swl.inc(64)

    sig_s = chain(p, "s", ss, [
        lambda e: e.activation(ab[:, 0:32], raw[:, 0:32], AF.Exp, bias=0.0, scale=-1.0),
    ], pre=lambda e: e.wait_ge(scl.h, c_rdy))
    sig_v = chain(p, "v", vs, [
        lambda e: e.tensor_scalar(ab[:, 0:32], ab[:, 0:32], 1.0, None, ALU.add),
        lambda e: e.reciprocal(ab[:, 0:32], ab[:, 0:32]),
        lambda e: e.tensor_scalar(ab[:, 16:32], ab[:, 16:32], 0.5, None, ALU.mult),
        lambda e: e.tensor_scalar(ab[:, 0:16], ab[:, 0:16], -0.5, None, ALU.mult),
    ], pre=lambda e, w=sig_s: e.wait_ge(ss.h, w))
    p.t(lambda e, w=sig_v: (e.wait_ge(vs.h, w),
        e.matmul(pg[:, 0:32], onesr[:, :], ab[:, 0:32], start=True, stop=True).then_inc(ts.h, 1)))
    rep_mm = ts.inc(1)
    ab_rdy = chain(p, "v", vs, [
        lambda e: e.tensor_copy(alR[:, 0:32], pg[:, 0:32]),
        lambda e: e.memset(SMall[:, :], 0.0),
    ], pre=lambda e, w=rep_mm: e.wait_ge(ts.h, w))

    prev = {"xnf_v": None, "xd_st": 0, "rt_s": None, "cw_v": ab_rdy, "pt_v": None, "pt_s": None}
    for i in range(NT):
        slot = i % 2
        xti = xt[:, slot * 1024:(slot + 1) * 1024]

        def xload(e, i=i, xti=xti, slot=slot, w=prev["xnf_v"], w2=prev["rt_s"]):
            if w:
                e.wait_ge(ts.h, w)
            e.dma_start(out=xti, in_=io["xn1s"][i * 128:(i + 1) * 128, :]).then_inc(sxl.h, 16)
        p.y(xload)
        ld = sxl.inc(16)

        tr_d = {}
        ev_v, ev_s = {}, {}
        for a in range(8):
            pk = [pf[2][:, 0:128], pf[3][:, 0:128]][a % 2]

            def trp(e, a=a, i=i, pk=pk, ld=ld, slot=slot, ev_v=ev_v, ev_s=ev_s,
                    pv=prev["pt_v"], psv=prev["pt_s"], first=(i == 0)):
                if a == 0:
                    e.wait_ge(sxl.h, ld)
                    if first:
                        e.wait_ge(scl.h, c_rdy)
                if a >= 2:
                    e.wait_ge(vs.h, ev_v[a - 2])
                    e.wait_ge(ss.h, ev_s[a - 2])
                elif pv:
                    e.wait_ge(vs.h, pv)
                    e.wait_ge(ss.h, psv)
                e.transpose(pk, xt[:, slot * 1024 + a * 128: slot * 1024 + (a + 1) * 128],
                            idf[:, :]).then_inc(ts.h, 1)
            p.t(trp)
            trd = ts.inc(1)
            ev_v[a] = chain(p, "v", vs, [
                lambda e, a=a, pk=pk: e.tensor_copy(xnTf[:, a * 128:(a + 1) * 128], pk)],
                pre=lambda e, w=trd: e.wait_ge(ts.h, w))
            ev_s[a] = chain(p, "s", ss, [
                lambda e, a=a: e.copy(xdstg[:, a * 128:(a + 1) * 128],
                                      xnTf[:, a * 128:(a + 1) * 128])],
                pre=lambda e, w=ev_v[a], wst=prev["xd_st"]: (
                    e.wait_ge(vs.h, w), wst and e.wait_ge(sxd.h, wst)))
        prev["pt_v"] = ev_v[7]
        prev["pt_s"] = ev_s[7]

        def xdstore(e, i=i, w=ev_s[7]):
            e.wait_ge(ss.h, w)
            e.dma_start(out=io["xn1sd"].rearrange("a p n -> p a n")[:, :, i * 128:(i + 1) * 128],
                        in_=xdstg.rearrange("p (a n) -> p a n", a=8)).then_inc(sxd.h, 16)
        p.y(xdstore)
        prev["xd_st"] = sxd.inc(16)

        def rmm(e, i=i, allv=ev_v[7], w=prev["cw_v"]):
            e.wait_ge(vs.h, allv)
            if w:
                e.wait_ge(vs.h, w)
            for a in range(8):
                ins = e.matmul(pg[:, 0:8], xnTf[:, a * 128:(a + 1) * 128], gT[:, a * 8:(a + 1) * 8],
                               start=(a == 0), stop=(a == 7))
            ins.then_inc(ts.h, 1)
        p.t(rmm)
        mm_d = ts.inc(1)

        lb_v = chain(p, "v", vs, [
            lambda e: e.tensor_copy(lbuf[:, :], pg[:, 0:8]),
            lambda e: e.reduce_max(mwork[:, 0:1], lbuf[:, :], axis=AX.X, negate=True),
        ], pre=lambda e, w=mm_d, w2=prev["rt_s"]: (
            e.wait_ge(ts.h, w), w2 and e.wait_ge(ss.h, w2)))
        prev["cw_v"] = lb_v
        e_s = chain(p, "s", ss, [
            lambda e: e.activation(ebuf[:, :], lbuf[:, :], AF.Exp,
                                   bias=mwork[:, 0:1], scale=1.0),
        ], pre=lambda e, w=lb_v: e.wait_ge(vs.h, w))
        prev["rt_s"] = e_s
        chain(p, "v", vs, [
            lambda e: e.reduce_max(mwork[:, 1:2], ebuf[:, :], axis=AX.X),
            lambda e: e.tensor_single_scalar(mwork[:, 4:12], ebuf[:, :], mwork[:, 1:2], ALU.is_ge),
            lambda e: e.scalar_tensor_tensor(mwork[:, 4:12], mwork[:, 4:12], -1e30, ebuf[:, :],
                                             ALU.mult, ALU.add),
            lambda e: e.reduce_max(mwork[:, 2:3], mwork[:, 4:12], axis=AX.X),
            lambda e: e.tensor_single_scalar(mwork[:, 4:12], ebuf[:, :], mwork[:, 2:3], ALU.is_ge),
            lambda e: e.tensor_tensor(mwork[:, 3:4], mwork[:, 1:2], mwork[:, 2:3], ALU.add),
            lambda e: e.reciprocal(mwork[:, 3:4], mwork[:, 3:4]),
            lambda e: e.tensor_tensor(mwork[:, 12:20], ebuf[:, :], mwork[:, 4:12], ALU.mult),
            lambda e, i=i: e.tensor_scalar(cw[:, i * 8:(i + 1) * 8], mwork[:, 12:20],
                                           mwork[:, 3:4], None, ALU.mult),
        ], pre=lambda e, w=e_s: e.wait_ge(ss.h, w))
        prev["xnf_v"] = ts.n

    barrier(C, p)

    SM_ev = {}
    mn = {"xn_use": None, "pp_v": None, "pp_s": None, "hd_pg_v": None,
          "ktr_v": None, "sq_evt": None, "st_cnt": 0, "s1_v": None}
    for c in range(NCH):
        slot = c % 2

        def cload(e, c=c, slot=slot, w=mn["xn_use"]):
            if w:
                e.wait_ge(ts.h, w)
            e.dma_start(out=xnc[:, slot * 1024:(slot + 1) * 1024].rearrange("p (a n) -> p a n", a=8),
                        in_=io["xn1sd"].rearrange("a p n -> p a n")[:, :, c * 128:(c + 1) * 128]
                        ).then_inc(sxc.h, 16)
        p.y(cload)
        cld = sxc.inc(16)

        kn_v = None
        for pi, pofs in ((0, 0), (1, 8192), (2, 16384)):
            def mmP(e, pi=pi, pofs=pofs, c=c, slot=slot, cld=cld,
                    ppv=mn["pp_v"], pps=mn["pp_s"], w_rdy=w_rdy):
                if pi == 0:
                    e.wait_ge(sxc.h, cld)
                    if c == 0:
                        e.wait_ge(swl.h, w_rdy)
                        e.wait_ge(vs.h, ab_rdy)
                if ppv:
                    e.wait_ge(vs.h, ppv)
                if pps:
                    e.wait_ge(ss.h, pps)
                for half in range(2):
                    for k in range(8):
                        ins = e.matmul([pp0, pp1][half][:, :],
                                       xnc[:, slot * 1024 + k * 128:slot * 1024 + (k + 1) * 128],
                                       wqkv[:, pofs + k * 1024 + half * 512: pofs + k * 1024 + half * 512 + 512],
                                       start=(k == 0), stop=(k == 7))
                ins.then_inc(ts.h, 1)
            p.t(mmP)
            pd = ts.inc(1)

            if pi == 0:
                mn["pp_s"] = chain(p, "s", ss, [
                    lambda e: e.copy(Qsb[:, 0:512], pp0[:, :]),
                    lambda e: e.copy(Qsb[:, 512:1024], pp1[:, :]),
                ], pre=lambda e, w=pd: e.wait_ge(ts.h, w))
                mn["q_s"] = mn["pp_s"]
                mn["pp_v"] = None
            elif pi == 1:
                kn_v = chain(p, "v", vs, [
                    lambda e: e.tensor_copy(Kf[:, 0:512], pp0[:, :]),
                    lambda e: e.tensor_copy(Kf[:, 512:1024], pp1[:, :]),
                    lambda e: e.tensor_tensor(Ksq[:, :], Kf[:, :], Kf[:, :], ALU.mult),
                    lambda e: e.reduce_sum(kst[:, 0:16].rearrange("p (h o) -> p h o", h=16),
                                           Ksq.rearrange("p (h d) -> p h d", h=16)[:, :, :],
                                           axis=AX.X),
                    lambda e: e.tensor_scalar_max(kst[:, 0:16], kst[:, 0:16], 1e-24),
                ], pre=lambda e, w=pd: e.wait_ge(ts.h, w))
                mn["pp_v"] = kn_v
                mn["pp_s"] = None
            else:
                rk_s = chain(p, "s", ss, [
                    lambda e: e.activation(kst[:, 16:32], kst[:, 0:16], AF.Ln, bias=0.0, scale=1.0),
                    lambda e: e.activation(kst[:, 16:32], kst[:, 16:32], AF.Exp, bias=0.0, scale=-0.5),
                ], pre=lambda e, w=kn_v: e.wait_ge(vs.h, w))
                vv = chain(p, "v", vs, [
                    (lambda e, h=h: e.tensor_scalar(
                        khvb[:, h * 128:h * 128 + 64], Kf[:, h * 64:(h + 1) * 64],
                        kst[:, 16 + h:17 + h], None, ALU.mult)) for h in range(16)
                ] + [
                    (lambda e, h=h: e.tensor_scalar(
                        khvb[:, h * 128 + 64:(h + 1) * 128],
                        [pp0, pp1][h // 8][:, (h % 8) * 64:(h % 8 + 1) * 64],
                        alR[:, 16 + h:17 + h], None, ALU.mult)) for h in range(16)
                ], pre=lambda e, w=rk_s, w2=pd: (e.wait_ge(ss.h, w), e.wait_ge(ts.h, w2)))
                mn["pp_v"] = vv
                mn["pp_s"] = None
        khvb_rdy = mn["pp_v"]
        q_rdy = ts.n

        def hgen(c, h, W):
            Rh = khvb[:, h * 128:(h + 1) * 128]
            nal = alR[:, h:h + 1]
            pgW, pabW, prW, ptrW = W["pg"], W["pab"], W["pr"], W["ptr"]
            ce = W["ce"]
            csem = vs if ce == "v" else ss
            cp = (lambda e, o, i: e.tensor_copy(o, i)) if ce == "v" else (lambda e, o, i: e.copy(o, i))

            def t_kh(e, h=h, w=khvb_rdy, pv=PTRS[W["pb"]]):
                e.wait_ge(vs.h, w)
                if pv:
                    e.wait_ge(pv[0], pv[1])
                e.transpose(ptrW[0:64, :], khvb[:, h * 128:h * 128 + 64], idb[:, :]).then_inc(ts.h, 1)
            p.t(t_kh)
            tk = ts.inc(1)
            khT_v = chain(p, ce, csem, [
                lambda e: cp(e, W["khT"][:, :], ptrW[0:64, :])],
                pre=lambda e, w=tk: e.wait_ge(ts.h, w))
            PTRS[W["pb"]] = (csem.h, khT_v)
            yield

            def t_q(e, h=h, w=khT_v, qs=mn["q_s"], pv=PTRS[W["pb"]]):
                e.wait_ge(csem.h, w)
                if pv:
                    e.wait_ge(pv[0], pv[1])
                if qs:
                    e.wait_ge(ss.h, qs)
                e.transpose(ptrW[0:64, :], Qsb[:, h * 64:(h + 1) * 64], idb[:, :]).then_inc(ts.h, 1)
            p.t(t_q)
            tq = ts.inc(1)
            qT_v = chain(p, ce, csem, [
                lambda e: cp(e, W["qT"][:, :], ptrW[0:64, :])],
                pre=lambda e, w=tq: e.wait_ge(ts.h, w))
            PTRS[W["pb"]] = (csem.h, qT_v)
            yield

            def mmG(e, w=khT_v, pv=W["pg_v"]):
                e.wait_ge(csem.h, w)
                if pv:
                    e.wait_ge(csem.h, pv)
                e.matmul(pgW, W["khT"][:, :], W["khT"][:, :], start=True, stop=True).then_inc(ts.h, 1)
            p.t(mmG)
            gmm = ts.inc(1)
            gsc = chain(p, "s", ss, [
                lambda e, nal=nal: e.activation(W["tmpf"][:, :], pgW, AF.Copy, scale=nal)],
                pre=lambda e, w=gmm: e.wait_ge(ts.h, w))
            nmask_v = chain(p, "v", vs, [
                lambda e: e.tensor_tensor(W["nb"][:, 0:128], W["tmpf"][:, :], msl[:, :], ALU.mult),
                lambda e: e.tensor_tensor(W["nb"][:, 128:256], W["tmpf"][:, :], msu[:, :], ALU.mult),
            ], pre=lambda e, w=gsc: e.wait_ge(ss.h, w))
            yield

            def mmSq(e, w=qT_v, w2=nmask_v):
                e.wait_ge(vs.h, w)
                e.wait_ge(vs.h, w2)
                e.matmul(pgW, W["khT"][:, :], W["qT"][:, :], start=True, stop=True).then_inc(ts.h, 1)
            p.t(mmSq)
            sqmm = ts.inc(1)
            sq_v = chain(p, "v", vs, [
                lambda e: e.tensor_tensor(W["sqT"][:, :], pgW, msu[:, :], ALU.mult)],
                pre=lambda e, w=sqmm: e.wait_ge(ts.h, w))
            yield

            npair = 0
            rcur, rnxt = Rh, W["ra"]
            r_ev = nmask_v
            sq_evt = None
            for j in range(7):
                def mmR(e, npair=npair, rcur=rcur, w=r_ev, sqv=sq_evt, sqvv=sq_v, j=j):
                    e.wait_ge((vs if j == 0 else csem).h, w)
                    if sqv:
                        e.wait_ge(csem.h, sqv)
                    if j == 0:
                        e.wait_ge(vs.h, sqvv)
                    e.matmul(prW, idb[:, :], rcur[:, :], start=True, stop=False)
                    e.matmul(prW, W["nb"][:, npair + 128:npair + 256], rcur[:, :],
                             start=False, stop=True).then_inc(ts.h, 1)
                p.t(mmR)
                rmm_d = ts.inc(1)
                r_ev = chain(p, ce, csem, [
                    lambda e, rnxt=rnxt: cp(e, rnxt[:, :], prW)],
                    pre=lambda e, w=rmm_d: e.wait_ge(ts.h, w))
                if j < 6:
                    def mmN(e, npair=npair, w=r_ev, sqv=sq_evt, svv=sq_v):
                        e.wait_ge(csem.h, w)
                        if sqv:
                            e.wait_ge(csem.h, sqv)
                        else:
                            e.wait_ge(vs.h, svv)
                        e.matmul(pabW[:, 0:128], W["nb"][:, npair + 128:npair + 256],
                                 W["nb"][:, npair:npair + 128], start=True, stop=False)
                        e.matmul(pabW[:, 128:256], W["nb"][:, npair:npair + 128],
                                 W["nb"][:, npair + 128:npair + 256], start=False, stop=True).then_inc(ts.h, 1)
                    p.t(mmN)
                    nmm_d = ts.inc(1)
                    npair = 256 - npair
                    sq_evt = chain(p, ce, csem, [
                        lambda e, npair=npair: cp(e, W["nb"][:, npair:npair + 256], pabW)],
                        pre=lambda e, w=nmm_d: e.wait_ge(ts.h, w))
                rcur, rnxt = rnxt, (W["rb"] if rnxt is W["ra"] else W["ra"])
                yield
            Rf = rcur

            def mmQt(e, w=r_ev, w2=sq_v, sqv=sq_evt):
                e.wait_ge(csem.h, w)
                e.wait_ge(vs.h, w2)
                e.wait_ge(csem.h, sqv)
                e.matmul(pgW[:, 0:64], W["sqT"][:, :], Rf[:, 0:64], start=True, stop=True).then_inc(ts.h, 1)
            p.t(mmQt)
            qtmm = ts.inc(1)
            qt_s = chain(p, "s", ss, [
                lambda e, nal=nal: e.activation(W["tmq"][:, :], pgW[:, 0:64], AF.Copy, scale=nal)],
                pre=lambda e, w=qtmm: e.wait_ge(ts.h, w))
            qt_v = chain(p, "v", vs, [
                lambda e, h=h: e.tensor_tensor(W["Qt"][:, :], W["tmq"][:, :],
                                               Qsb[:, h * 64:(h + 1) * 64], ALU.add)],
                pre=lambda e, w=qt_s: e.wait_ge(ss.h, w))
            yield

            def t_qt(e, w=qt_v, pv=PTRS[W["pb"]]):
                e.wait_ge(vs.h, w)
                if pv:
                    e.wait_ge(pv[0], pv[1])
                e.transpose(ptrW[0:64, :], W["Qt"][:, :], idb[:, :]).then_inc(ts.h, 1)
            p.t(t_qt)
            tqt = ts.inc(1)
            qtT_v = chain(p, ce, csem, [
                lambda e: cp(e, W["qtT"][:, :], ptrW[0:64, :])],
                pre=lambda e, w=tqt: e.wait_ge(ts.h, w))
            PTRS[W["pb"]] = (csem.h, qtT_v)
            yield

            def mmY(e, h=h, w=qtT_v, Rf=Rf, c=c, smw=SM_ev.get((c - 1, h))):
                e.wait_ge(csem.h, w)
                if smw:
                    e.wait_ge(csem.h, smw)
                e.matmul(pgW[0:64, :], SMall[:, h * 64:(h + 1) * 64], W["qtT"][:, :],
                         start=True, stop=False)
                e.matmul(pgW[0:64, :], Rf[:, 64:128], W["sqT"][:, :],
                         start=False, stop=True).then_inc(ts.h, 1)
            p.t(mmY)
            ymm = ts.inc(1)
            y_v = chain(p, ce, csem, [
                lambda e, h=h: cp(
                    e, Ych[(h % 2) * 64:(h % 2) * 64 + 64, (h // 2) * 128:(h // 2 + 1) * 128],
                    pgW[0:64, :])],
                pre=lambda e, w=ymm: e.wait_ge(ts.h, w))
            yield

            def mmA2(e, w=y_v, Rf=Rf, h=h):
                e.wait_ge(csem.h, w)
                e.matmul(pgW[0:64, 0:64], Rf[:, 0:64], khvb[:, h * 128:h * 128 + 64],
                         start=True, stop=True).then_inc(ts.h, 1)
            p.t(mmA2)
            amm = ts.inc(1)
            a_s = chain(p, "s", ss, [
                lambda e, nal=nal: e.activation(W["tmA"][:, :], pgW[0:64, 0:64], AF.Copy,
                                                scale=nal[0:64, :])],
                pre=lambda e, w=amm: e.wait_ge(ts.h, w))
            a_v = chain(p, "v", vs, [
                lambda e: e.tensor_tensor(W["Asb"][:, :], W["tmA"][:, :], idb[0:64, 0:64], ALU.add)],
                pre=lambda e, w=a_s: e.wait_ge(ss.h, w))
            yield

            def mmSM(e, h=h, w=a_v, Rf=Rf):
                e.wait_ge(vs.h, w)
                e.matmul(pgW[0:64, 0:64], W["Asb"][:, :], SMall[:, h * 64:(h + 1) * 64],
                         start=True, stop=False)
                e.matmul(pgW[0:64, 0:64], khvb[:, h * 128:h * 128 + 64], Rf[:, 64:128],
                         start=False, stop=True).then_inc(ts.h, 1)
            p.t(mmSM)
            smmm = ts.inc(1)
            SM_ev[(c, h)] = chain(p, ce, csem, [
                lambda e, h=h: cp(e, SMall[:, h * 64:(h + 1) * 64], pgW[0:64, 0:64])],
                pre=lambda e, w=smmm: e.wait_ge(ts.h, w))
            W["pg_v"] = SM_ev[(c, h)]
            yield

        for hp4 in range(4):
            alive = [hgen(c, 4 * hp4 + wi, WAYS[wi]) for wi in range(4)]
            while alive:
                nxt = []
                for g in alive:
                    try:
                        next(g)
                        nxt.append(g)
                    except StopIteration:
                        pass
                alive = nxt

        yl_v = max(SM_ev[(c, 12)], SM_ev[(c, 14)])
        yl_s = max(SM_ev[(c, 13)], SM_ev[(c, 15)])

        def mmW(e, c=c, wv=yl_v, ws=yl_s, pv=mn["s1_v"]):
            e.wait_ge(vs.h, wv)
            e.wait_ge(ss.h, ws)
            if pv:
                e.wait_ge(vs.h, pv)
            for half in range(2):
                for k in range(8):
                    ins = e.matmul([pp0, pp1][half][:, :], Ych[:, k * 128:(k + 1) * 128],
                                   wpj[:, k * 1024 + half * 512: k * 1024 + half * 512 + 512],
                                   start=(k == 0), stop=(k == 7))
            ins.then_inc(ts.h, 1)
        p.t(mmW)
        wmm = ts.inc(1)

        def preS1(e, w=wmm, c=c, slot=slot):
            e.wait_ge(ts.h, w)
            if c >= 2:
                e.wait_ge(sstt[slot].h, 16 * (c // 2))
        s1v = chain(p, "v", vs, [
            lambda e, slot=slot: e.tensor_copy(s1stg[:, slot * 1024:slot * 1024 + 512], pp0[:, :]),
            lambda e, slot=slot: e.tensor_copy(s1stg[:, slot * 1024 + 512:slot * 1024 + 1024], pp1[:, :]),
        ], pre=preS1)
        mn["s1_v"] = s1v
        mn["pp_v"] = s1v
        mn["pp_s"] = None

        def s1store(e, c=c, slot=slot, w=s1v):
            e.wait_ge(vs.h, w)
            e.dma_start(out=io["s1s"][c, :, :],
                        in_=s1stg[:, slot * 1024:(slot + 1) * 1024]).then_inc(sstt[slot].h, 16)
        p.y(s1store)
        sstt[slot].inc(16)
        mn["xn_use"] = ts.n
    p.emit()



def ph_ffn1(C, es):
    nc, io, sb = C.nc, C.io, C.sb
    p = P(C)
    ts, vs, ss = C.sem("ts"), C.sem("vs"), C.sem("ss")
    sxn = C.sem("f1_xn")
    sgu = [C.sem("f1_gu0"), C.sem("f1_gu1")]
    sdw = C.sem("f1_dw")
    sxl = C.sem("f1_x")
    ssl = C.sem("f1_s1")
    sso = C.sem("f1_out")
    cw = sb["combine1w"]

    xn_tc = es.enter_context(nc.sbuf_tensor("f1_xn", [128, 8192], BF16))
    he1 = es.enter_context(nc.sbuf_tensor("f1_he", [128, 32768], BF16))
    dT = es.enter_context(nc.sbuf_tensor("f1_dT", [128, 32768], BF16))
    wsl = es.enter_context(nc.sbuf_tensor("f1_wsl", [128, 2 * 2048], BF16))
    f1acc = es.enter_context(nc.sbuf_tensor("f1_acc", [128, 8192], F32))
    gel = es.enter_context(nc.sbuf_tensor("f1_gel", [128, 2048], BF16))
    tmc = es.enter_context(nc.sbuf_tensor("f1_tmc", [128, 1024], F32))
    g1b = es.enter_context(nc.sbuf_tensor("f1_g1", [128, 1024], BF16))
    xt = es.enter_context(nc.sbuf_tensor("f1_xt", [128, 1024], F32))
    s1t = es.enter_context(nc.sbuf_tensor("f1_s1t", [128, 1024], BF16))
    PB = [es.enter_context(nc.psum_tensor(f"f1_pb{j}", [128, 512], F32)) for j in range(8)]

    prev = {"tc_ts": None, "gu_use": {}, "dT_use": None,
            "gu_ev": {"v": None, "s": None},
            "pd_v": {}, "E_ss": None, "E_vs": None, "x1_v": None, "o_st": 0}
    for tc in range(4):
        def xload(e, tc=tc, w=prev["tc_ts"]):
            if w:
                e.wait_ge(ts.h, w)
            e.dma_start(out=xn_tc.rearrange("p (a n) -> p a n", a=8),
                        in_=io["xn1sd"].rearrange("a p n -> p a n")[:, :, tc * 1024:(tc + 1) * 1024]
                        ).then_inc(sxn.h, 16)
        p.y(xload)
        xn_rdy = sxn.inc(16)

        for ex in range(8):
            def dload(e, ex=ex, w=prev["dT_use"]):
                if w:
                    e.wait_ge(ts.h, w)
                e.dma_start(out=dT.rearrange("p (kk n) -> p kk n", kk=32),
                            in_=io["downT"][ex].rearrange("(kk p) n -> p kk n", p=128)
                            ).then_inc(sdw.h, 16)
            p.g(dload)
            dld = sdw.inc(16)

            gu_ev = prev["gu_ev"]
            gu_ev2 = prev.get("gu_ev2", {"v": None, "s": None})
            for fr in range(32):
                slot = fr % 2
                pg1 = [PB[4 * slot + 0], PB[4 * slot + 1]]
                pu1 = [PB[4 * slot + 2], PB[4 * slot + 3]]

                def gload(e, ex=ex, fr=fr, slot=slot, w=prev["gu_use"].get(slot)):
                    if w:
                        e.wait_ge(ts.h, w)
                    e.dma_start(out=wsl[:, slot * 2048:slot * 2048 + 1024].rearrange("p (k n) -> p k n", k=8),
                                in_=io["gupT"][ex].rearrange("(k p) n -> p k n", p=128)[:, :, fr * 128:(fr + 1) * 128]
                                ).then_inc(sgu[slot].h, 16)
                    e.dma_start(out=wsl[:, slot * 2048 + 1024:(slot + 1) * 2048].rearrange("p (k n) -> p k n", k=8),
                                in_=io["gupT"][ex].rearrange("(k p) n -> p k n", p=128)[:, :, 4096 + fr * 128:4096 + (fr + 1) * 128]
                                ).then_inc(sgu[slot].h, 16)
                p.g(gload)
                gld = sgu[slot].inc(32)

                def mmGU(e, fr=fr, slot=slot, gld=gld, xn_rdy=xn_rdy, pg1=pg1, pu1=pu1,
                         gev=dict(gu_ev2), ex=ex, tc=tc, pdv=dict(prev["pd_v"])):
                    e.wait_ge(sgu[slot].h, gld)
                    if fr == 0 and ex == 0:
                        e.wait_ge(sxn.h, xn_rdy)
                    if fr == 0 and pdv:
                        e.wait_ge(vs.h, max(pdv.values()))
                    if gev["v"]:
                        e.wait_ge(vs.h, gev["v"])
                        e.wait_ge(ss.h, gev["s"])
                    for wofs, pp in ((0, pg1), (1024, pu1)):
                        for tn in range(2):
                            for k in range(8):
                                ins = e.matmul(pp[tn][:, :],
                                               wsl[:, slot * 2048 + wofs + k * 128: slot * 2048 + wofs + (k + 1) * 128],
                                               xn_tc[:, k * 1024 + tn * 512: k * 1024 + (tn + 1) * 512],
                                               start=(k == 0), stop=(k == 7))
                    ins.then_inc(ts.h, 1)
                p.t(mmGU)
                mmd = ts.inc(1)
                prev["gu_use"][slot] = mmd

                sG = chain(p, "s", ss, [
                    lambda e, pg1=pg1, slot=slot: e.activation(gel[:, slot * 1024:slot * 1024 + 512],
                                                               pg1[0][:, :], AF.Gelu),
                    lambda e, pg1=pg1, slot=slot: e.activation(gel[:, slot * 1024 + 512:slot * 1024 + 1024],
                                                               pg1[1][:, :], AF.Gelu),
                ], pre=lambda e, w=mmd: e.wait_ge(ts.h, w))
                vG = chain(p, "v", vs, [
                    lambda e, fr=fr, pu1=pu1, slot=slot: e.tensor_tensor(
                        he1[:, fr * 1024:fr * 1024 + 512],
                        gel[:, slot * 1024:slot * 1024 + 512], pu1[0][:, :], ALU.mult),
                    lambda e, fr=fr, pu1=pu1, slot=slot: e.tensor_tensor(
                        he1[:, fr * 1024 + 512:(fr + 1) * 1024],
                        gel[:, slot * 1024 + 512:slot * 1024 + 1024], pu1[1][:, :], ALU.mult),
                ], pre=lambda e, w=sG: e.wait_ge(ss.h, w))
                gu_ev2 = gu_ev
                gu_ev = {"v": vG, "s": sG}
            prev["gu_ev"] = gu_ev
            prev["gu_ev2"] = gu_ev2
            he_rdy = gu_ev["v"]

            for tsub in range(8):
                for half in range(2):
                    pdk = PB[half]

                    def mmD(e, tsub=tsub, half=half, pdk=pdk, dld=dld, he_rdy=he_rdy,
                            he_rdy2=gu_ev2["v"],
                            pv=prev["pd_v"].get(half), ess=prev["E_ss"], evs=prev["E_vs"],
                            first=(tsub == 0 and half == 0), ex=ex):
                        e.wait_ge(sdw.h, dld)
                        if first:
                            e.wait_ge(vs.h, he_rdy)
                            if he_rdy2:
                                e.wait_ge(vs.h, he_rdy2)
                            if ex == 0 and ess:
                                e.wait_ge(ss.h, ess)
                                e.wait_ge(vs.h, evs)
                        if pv:
                            e.wait_ge(vs.h, pv)
                        for kk in range(32):
                            ins = e.matmul(pdk[:, :],
                                           he1[:, kk * 1024 + tsub * 128: kk * 1024 + tsub * 128 + 128],
                                           dT[:, kk * 1024 + half * 512: kk * 1024 + (half + 1) * 512],
                                           start=(kk == 0), stop=(kk == 31))
                        ins.then_inc(ts.h, 1)
                    p.t(mmD)
                    mmd = ts.inc(1)

                    gcol = (tc * 8 + tsub) * 8 + ex
                    sD = chain(p, "s", ss, [
                        lambda e, gcol=gcol, half=half, pdk=pdk: e.activation(
                            tmc[:, half * 512:(half + 1) * 512], pdk[:, :], AF.Copy,
                            scale=cw[:, gcol:gcol + 1])],
                        pre=lambda e, w=mmd: e.wait_ge(ts.h, w))
                    if ex == 0:
                        prev["pd_v"][half] = chain(p, "v", vs, [
                            lambda e, tsub=tsub, half=half: e.tensor_copy(
                                f1acc[:, tsub * 1024 + half * 512: tsub * 1024 + (half + 1) * 512],
                                tmc[:, half * 512:(half + 1) * 512])],
                            pre=lambda e, w=sD: e.wait_ge(ss.h, w))
                    else:
                        prev["pd_v"][half] = chain(p, "v", vs, [
                            lambda e, tsub=tsub, half=half: e.tensor_tensor(
                                f1acc[:, tsub * 1024 + half * 512: tsub * 1024 + (half + 1) * 512],
                                f1acc[:, tsub * 1024 + half * 512: tsub * 1024 + (half + 1) * 512],
                                tmc[:, half * 512:(half + 1) * 512], ALU.add)],
                            pre=lambda e, w=sD: e.wait_ge(ss.h, w))
            prev["dT_use"] = ts.n
        routed_done = prev["pd_v"][1]

        e_ss = None
        e_vs = None
        for tsub in range(8):
            cid = tc * 8 + tsub

            def eload(e, cid=cid, w=prev["x1_v"], ost=prev["o_st"]):
                if w:
                    e.wait_ge(vs.h, w)
                if ost:
                    e.wait_ge(sso.h, ost)
                e.dma_start(out=xt[:, :], in_=io["x1s"][cid * 128:(cid + 1) * 128, :]).then_inc(sxl.h, 16)
                e.dma_start(out=s1t[:, :], in_=io["s1s"][cid, :, :]).then_inc(ssl.h, 16)
            p.y(eload)
            xld = sxl.inc(16)
            s1ld = ssl.inc(16)

            e_ss = chain(p, "s", ss, [
                lambda e, tsub=tsub: e.activation(g1b[:, :], f1acc[:, tsub * 1024:(tsub + 1) * 1024],
                                                  AF.Silu)],
                pre=lambda e, w=routed_done, tsub=tsub, pv=prev["x1_v"]: (
                    tsub == 0 and e.wait_ge(vs.h, w),
                    pv and e.wait_ge(vs.h, pv)))

            def preE(e, xld=xld, s1ld=s1ld, w=e_ss, ost=prev["o_st"]):
                e.wait_ge(sxl.h, xld)
                e.wait_ge(ssl.h, s1ld)
                e.wait_ge(ss.h, w)
                if ost:
                    e.wait_ge(sso.h, ost)
            e_vs = chain(p, "v", vs, [
                lambda e: e.tensor_tensor(tmc[:, :], s1t[:, :], g1b[:, :], ALU.mult),
                lambda e: e.tensor_tensor(xt[:, :], xt[:, :], tmc[:, :], ALU.add),
            ], pre=preE)
            prev["x1_v"] = e_vs

            def ostore(e, cid=cid, w=e_vs):
                e.wait_ge(vs.h, w)
                e.dma_start(out=io["out"][cid * 128:(cid + 1) * 128, :], in_=xt[:, :]).then_inc(sso.h, 16)
            p.y(ostore)
            prev["o_st"] = sso.inc(16)
        prev["E_ss"] = e_ss
        prev["E_vs"] = e_vs
        prev["tc_ts"] = ts.n
    p.emit()



WSHAPES = {
    "Wq0": [DM, DM], "Wk0": [DM, DM], "Wv0": [DM, 2048], "Wg0": [DM, 2048],
    "Wo0": [2048, DM], "gn_w": [1, 512], "w12T": [8, DM, 2048], "w3T": [8, DM, DM],
    "w1T": [DM, T], "w2T": [DM, T], "w3sT": [T, DM], "routerT": [DM, 8],
    "ebias": [1, 8], "Wq1T": [DM, DM], "Wk1T": [DM, DM], "Wv1T": [DM, DM],
    "Wp1T": [DM, DM], "araw": [1, 16], "braw": [1, 16], "gateT": [DM, 8],
    "gupT": [8, DM, 8192], "downT": [8, T, DM],
}

def scratch_shapes():
    return {
        "qs": ([NCH, 128, 1024], BF16),
        "ks": ([NCH, 128, 1024], BF16),
        "vsd": ([NCH, 128, 2048], BF16),
        "gsd": ([NCH, 128, 2048], BF16),
        "s0s": ([NCH, 128, 1024], BF16),
        "x1s": ([T, DM], F32),
        "xn1s": ([T, DM], F32),
        "s1s": ([NCH, 128, 1024], BF16),
        "xns": ([8, 128, T], BF16),
        "xn1sd": ([8, 128, T], BF16),
    }

ALL_PHASES = ("ln0", "qk", "vg", "xdump", "ret", "ffn0", "l1a", "ffn1")


def build(dbg=(), phases=ALL_PHASES, final_out=False, scratch_in=()):
    from contextlib import ExitStack
    PHASES = {"ln0": ph_ln0, "qk": ph_qk, "vg": ph_vg, "xdump": ph_xdump,
              "ret": ph_ret, "ffn0": ph_ffn0, "l1a": ph_l1a, "ffn1": ph_ffn1}
    nc = bass.Bass("TRN2", target_bir_lowering=False)
    io = {}
    io["x"] = nc.dram_tensor("x", [T, DM], F32, kind="ExternalInput").ap()
    consts = host_constants()
    for k, v in consts.items():
        io[k] = nc.dram_tensor(k, list(v.shape), F32, kind="ExternalInput").ap()
    for k, shp in WSHAPES.items():
        io[k] = nc.dram_tensor(k, shp, F32, kind="ExternalInput").ap()
    for k, (shp, dt) in scratch_shapes().items():
        kind = ("ExternalInput" if k in scratch_in
                else "ExternalOutput" if k in dbg else "Internal")
        io[k] = nc.dram_tensor(k, shp, dt, kind=kind).ap()
    if final_out:
        io["out"] = nc.dram_tensor("out", [T, DM], F32, kind="ExternalOutput").ap()
    if "xn_dbg" in dbg:
        io["xn_dbg"] = nc.dram_tensor("xn_dbg", [8, 128, NT * 128], F32, kind="ExternalOutput").ap()
    if "gates_dbg" in dbg:
        io["gates_dbg"] = nc.dram_tensor("gates_dbg", [128, NT * 8], F32, kind="ExternalOutput").ap()
    if "cw_dbg" in dbg:
        io["cw_dbg"] = nc.dram_tensor("cw_dbg", [128, NT * 8], F32, kind="ExternalOutput").ap()
    if "cw_in" in scratch_in:
        io["cw_in"] = nc.dram_tensor("cw_in", [128, NT * 8], F32, kind="ExternalInput").ap()

    with ExitStack() as top:
        sb = {}
        sb["gates0"] = top.enter_context(nc.sbuf_tensor("gates0", [128, NT * 8], F32))
        sb["combine1w"] = top.enter_context(nc.sbuf_tensor("combine1w", [128, NT * 8], F32))
        sb["identF"] = top.enter_context(nc.sbuf_tensor("identF", [128, 128], F32))
        sb["identB"] = top.enter_context(nc.sbuf_tensor("identB", [128, 128], BF16))
        with nc.Block() as block:
            C = Ctx(nc, block, io, sb, top)
            s_init = C.sem("init")
            s_initg = C.sem("initg")
            p0 = P(C)
            p0.y(lambda e: e.dma_start(out=sb["identF"][:, :], in_=io["ident"][:, :]).then_inc(s_init.h, 16))
            if "cw_in" in io:
                p0.y(lambda e: e.dma_start(out=sb["combine1w"][:, :], in_=io["cw_in"][:, :]).then_inc(s_init.h, 16))
                s_init.inc(16)
            p0.g(lambda e: e.dma_start(out=sb["identB"][:, :], in_=io["ident"][:, :]).then_inc(s_initg.h, 16))
            s_init.inc(16)
            s_initg.inc(16)
            barrier(C, p0)
            p0.emit()

            XND_PHASES = {"ln0", "qk", "vg", "xdump"}
            xnd_scope = None
            for ph in phases:
                if ph in XND_PHASES and xnd_scope is None:
                    xnd_scope = ExitStack()
                    sb["xnD"] = xnd_scope.enter_context(nc.sbuf_tensor("xnD", [128, 8 * T], BF16))
                if ph not in XND_PHASES and xnd_scope is not None:
                    xnd_scope.close()
                    xnd_scope = None
                    sb.pop("xnD", None)
                with ExitStack() as es:
                    PHASES[ph](C, es)
                    pe = P(C)
                    barrier(C, pe)
                    pe.emit()
            if xnd_scope is not None:
                xnd_scope.close()
                sb.pop("xnD", None)

            pd = P(C)
            sdmp_g = C.sem("dumpg")
            sdmp_y = C.sem("dumpy")
            if "xn_dbg" in dbg:
                def dump_xn(e):
                    for a in range(8):
                        e.dma_start(out=io["xn_dbg"][a, :, :],
                                    in_=sb["xnD"][:, a * T:a * T + NT * 128]).then_inc(sdmp_g.h, 16)
                pd.g(dump_xn)
                sdmp_g.inc(16 * 8)
            if "gates_dbg" in dbg:
                pd.y(lambda e: e.dma_start(out=io["gates_dbg"][:, :], in_=sb["gates0"][:, :]).then_inc(sdmp_y.h, 16))
                sdmp_y.inc(16)
            if "cw_dbg" in dbg:
                pd.y(lambda e: e.dma_start(out=io["cw_dbg"][:, :], in_=sb["combine1w"][:, :]).then_inc(sdmp_y.h, 16))
                sdmp_y.inc(16)
            fin = [(s.h, s.n) for s in C.sem_list if s.n]

            def wait_all(e, fin=fin):
                for h, n in fin:
                    e.wait_ge(h, n)
            pd.y(wait_all)
            pd.emit()
    return nc


def make_in_map(b, x, w, consts):
    m = {"x": np.ascontiguousarray(x[b])}
    m.update(w)
    m.update(consts)
    return m


def kernel(**inputs):
    w = host_prep(inputs)
    consts = host_constants()
    x = np.ascontiguousarray(inputs["L0_attn_x"], dtype=np.float32)
    nc = build(dbg=(), phases=ALL_PHASES, final_out=True)
    in_maps = [make_in_map(b, x, w, consts) for b in range(8)]
    res = run_bass_kernel_spmd(nc, in_maps, core_ids=list(range(8)))
    return np.stack([res.results[b]["out"] for b in range(8)], axis=0)
